# version 02
# baseline (speedup 1.0000x reference)
import numpy as np
import ml_dtypes
import concourse.bass as bass
import concourse.mybir as mybir
from concourse.bass_utils import run_bass_kernel_spmd
from bass_rust import IndirectOffsetOnAxis
from contextlib import ExitStack

F32 = mybir.dt.float32
BF16 = mybir.dt.bfloat16
FP8 = mybir.dt.float8e4
I32 = mybir.dt.int32
I16 = mybir.dt.int16
AF = mybir.ActivationFunctionType
Alu = mybir.AluOpType

NCORES = 8
D = 1024
H = 4096
DCH = D // 128
HCH = H // 128
NBLK = 16
LN_EPS = 1e-5
QEPS = 1e-6
ACCW = 1088
INV_SQRT_C = 1.0 / 32.0
NEG_BIG = -1e30


class _Stop(Exception):
    pass


class Cfg:
    def __init__(self, T):
        self.T = T
        self.BLK = T // NBLK
        self.TPC = 2 * self.BLK
        self.NCH = self.TPC // 128
        self.NJC = self.BLK // 128
        self.ICOL = min(512, self.BLK)
        self.NIT = self.BLK // self.ICOL
        self.NIC = self.BLK // 128
        self.GRP = 2
        self.NG = self.NCH // self.GRP
        self.TG = 128 * self.GRP


class Sem:
    def __init__(self, nc, ctx, name):
        self.h = ctx.enter_context(nc.semaphore(name))
        self.n = 0

    def inc(self, instr, kk=1):
        instr.then_inc(self.h, kk)
        self.n += kk
        return self.n


class FenceSem(Sem):
    def __init__(self, nc, ctx, name, eng, wrap):
        super().__init__(nc, ctx, name)
        self._eng = eng
        self._wrap = wrap

    def inc(self, instr=None, kk=1):
        self._eng.drain()
        if self._wrap is not None:
            self._wrap._written.clear()
            self._wrap._read.clear()
        self._eng.nop().then_inc(self.h, kk)
        self.n += kk
        return self.n




class EngWrap:
    _OPS = ("tensor_scalar", "tensor_tensor", "scalar_tensor_tensor", "tensor_copy",
            "memset", "reciprocal", "tensor_reduce", "activation", "iota",
            "tensor_add", "tensor_sub", "tensor_mul", "memzero")

    def __init__(self, eng):
        self._e = eng
        self._written = set()
        self._read = set()

    def __getattr__(self, name):
        attr = getattr(self._e, name)
        if name not in self._OPS:
            return attr

        def call(*args, **kwargs):
            names_in = []
            names_out = []
            for i, a in enumerate(args):
                if hasattr(a, "tensor") and hasattr(a.tensor, "name"):
                    (names_out if i == 0 else names_in).append(a.tensor.name)
            for kk, a in kwargs.items():
                if hasattr(a, "tensor") and hasattr(a.tensor, "name"):
                    (names_out if kk == "accum_out" else names_in).append(a.tensor.name)
            if (any(n in self._written for n in names_in + names_out)
                    or any(n in self._read for n in names_out)):
                self._e.drain()
                self._written.clear()
                self._read.clear()
            r = attr(*args, **kwargs)
            self._written.update(names_out)
            self._read.update(names_in)
            return r

        return call


def build_program(cfg, stop_after="full"):
    nc = bass.Bass("TRN2", target_bir_lowering=False)
    BLK, TPC, NCH, GRP, TG, NG = cfg.BLK, cfg.TPC, cfg.NCH, cfg.GRP, cfg.TG, cfg.NG
    NJC, ICOL, NIT, NIC = cfg.NJC, cfg.ICOL, cfg.NIT, cfg.NIC

    def din(name, shape, dt=F32):
        return nc.dram_tensor(name, shape, dt, kind="ExternalInput").ap()

    x_in = din("x_in", [TPC, D])
    Wfc_d = din("Wfc", [H, D]); Wproj_d = din("Wproj", [D, H])
    Wk_d = din("Wk", [D, D]); Wq_d = din("Wq", [D, D]); Wv_d = din("Wv", [D, D]); Wp_d = din("Wp", [D, D])
    bk_d = din("bk", [1, D]); bq_d = din("bq", [1, D]); bv_d = din("bv", [1, D]); bp_d = din("bp", [1, D])
    lnw_d = din("lnw_t", [128, HCH]); lnb_d = din("lnb_t", [128, HCH])
    mvals_d = din("m_vals", [128, NCH], I32)
    dneg_d = din("dneg", [128, 17])
    qidx_d = din("qidx", [128, 17 * DCH], I32)
    vidx_d = din("vidx", [128, 17 * NIC], I32)
    kidx_d = din("kidx", [128, 17 * DCH], I32)
    aidx_d = din("aidx", [128, 17 * NIC], I32)
    y_out = nc.dram_tensor("y", [TPC, D], F32, kind="ExternalOutput").ap()

    x_hbm = nc.dram_tensor("x_hbm", [TPC, D], F32, kind="Internal")
    kT_hbm = nc.dram_tensor("kT_hbm", [2 * D, BLK], F32, kind="Internal")
    acc_hbm = nc.dram_tensor("acc_hbm", [TPC, ACCW], F32, kind="Internal")
    qb_in = nc.dram_tensor("qb_in", [2 * D, BLK], F32, kind="Internal")
    vb_in = nc.dram_tensor("vb_in", [TPC, D], F32, kind="Internal")
    qb_out = nc.dram_tensor("qb_out", [NCORES * 2 * D, BLK], F32, kind="Internal")
    vb_out = nc.dram_tensor("vb_out", [NCORES * TPC, D], F32, kind="Internal")
    wfcq_hbm = nc.dram_tensor("wfcq_hbm", [128, DCH * HCH * 128], FP8, kind="Internal")
    wprjq_hbm = nc.dram_tensor("wprjq_hbm", [128, HCH * D], BF16, kind="Internal")

    octx = ExitStack()
    E = octx.enter_context

    sync, gp_raw, te, ve_raw, ac_raw = nc.sync, nc.gpsimd, nc.tensor, nc.vector, nc.scalar
    ve = EngWrap(ve_raw)
    ac = EngWrap(ac_raw)
    gp = EngWrap(gp_raw)
    ts_, tt_, stt_ = ve.tensor_scalar, ve.tensor_tensor, ve.scalar_tensor_tensor

    LD = Sem(nc, octx, "LD")
    GD = Sem(nc, octx, "GD")
    PE = Sem(nc, octx, "PE")
    VE = FenceSem(nc, octx, "VE", ve_raw, ve)
    AC = FenceSem(nc, octx, "AC", ac_raw, ac)
    GP = Sem(nc, octx, "GP")
    CC = Sem(nc, octx, "CC")
    KQ = Sem(nc, octx, "KQ")
    VB = Sem(nc, octx, "VB")
    SC = Sem(nc, octx, "SC")
    ALLSEMS = (LD, GD, PE, VE, AC, GP, CC, KQ, VB, SC)

    def barrier():
        for eng in (sync, gp, te, ve, ac):
            for s in ALLSEMS:
                if s.n:
                    eng.wait_ge(s.h, s.n)

    def trans(out_ap, in_ap, first, last):
        p = in_ap.shape[0]
        return te.matmul(out_ap, in_ap, ident[0:p, 0:p],
                         start=first, stop=last, is_transpose=True)

    ones_col = E(nc.sbuf_tensor("ones_col", [128, 1], F32))
    ones_row = E(nc.sbuf_tensor("ones_row", [1, 128], F32))
    half_row = E(nc.sbuf_tensor("half_row", [1, 128], F32))
    ident = E(nc.sbuf_tensor("ident", [128, 128], F32))
    miota = E(nc.sbuf_tensor("miota", [128, 2 * BLK + 128], I16))
    s_dneg = E(nc.sbuf_tensor("s_dneg", [128, 17], F32))
    s_qidx = E(nc.sbuf_tensor("s_qidx", [128, 17 * DCH], I32))
    s_vidx = E(nc.sbuf_tensor("s_vidx", [128, 17 * NIC], I32))
    s_kidx = E(nc.sbuf_tensor("s_kidx", [128, 17 * DCH], I32))
    s_aidx = E(nc.sbuf_tensor("s_aidx", [128, 17 * NIC], I32))
    s_mv = E(nc.sbuf_tensor("s_mv", [128, NCH], I32))
    rot_c = E(nc.sbuf_tensor("rot_c", [128, NCH], F32))
    rot_s = E(nc.sbuf_tensor("rot_s", [128, NCH], F32))
    stepc = [E(nc.sbuf_tensor(f"stepc{j}", [128, NCH], F32)) for j in range(9)]
    steps = [E(nc.sbuf_tensor(f"steps{j}", [128, NCH], F32)) for j in range(9)]
    lnw_sb = E(nc.sbuf_tensor("lnw_sb", [128, HCH], F32))
    lnb_sb = E(nc.sbuf_tensor("lnb_sb", [128, HCH], F32))
    bk_b = E(nc.sbuf_tensor("bk_b", [128, D], F32))
    bq_b = E(nc.sbuf_tensor("bq_b", [128, D], F32))
    bv_b = E(nc.sbuf_tensor("bv_b", [128, D], F32))
    bp_b = E(nc.sbuf_tensor("bp_b", [128, D], F32))
    wW_b = E(nc.sbuf_tensor("wW_b", [128, D], F32))
    bW_b = E(nc.sbuf_tensor("bW_b", [128, D], F32))
    s_tau = E(nc.sbuf_tensor("s_tau", [128, 4], F32))
    svec = E(nc.sbuf_tensor("svec", [128, 64], F32))
    s_mu = E(nc.sbuf_tensor("s_mu", [128, NCH], F32))
    s_rs = E(nc.sbuf_tensor("s_rs", [128, NCH], F32))
    s_hmu = E(nc.sbuf_tensor("s_hmu", [128, NCH], F32))
    s_hrs = E(nc.sbuf_tensor("s_hrs", [128, NCH], F32))
    vt1 = E(nc.sbuf_tensor("vt1", [128, NCH], F32))
    vt2 = E(nc.sbuf_tensor("vt2", [128, NCH], F32))
    vt3 = E(nc.sbuf_tensor("vt3", [128, NCH], F32))
    bt_i = E(nc.sbuf_tensor("bt_i", [128, NCH], I32))
    bt_f = E(nc.sbuf_tensor("bt_f", [128, NCH], F32))

    pb = [E(nc.psum_tensor(f"pb{i}", [128, 512], F32)) for i in range(8)]
    E(nc.Block())

    def emit_rsqrt(out_ap, u_ap, w1, w2, iters=16):
        ts_(w1, u_ap, 1.0, 0.5, Alu.add, Alu.mult)
        for _ in range(iters):
            ve.reciprocal(w2, w1)
            tt_(w2, w2, u_ap, Alu.mult)
            tt_(w1, w1, w2, Alu.add)
            ts_(w1, w1, 0.5, None, Alu.mult)
        ve.reciprocal(out_ap, w1)

    def ffn_pass(wfcq_sb, wprjq_sb, write_y, tag):
        GRPl, TGl = GRP, TG
        with nc.sbuf_tensor(f"xg{tag}", [128, GRPl * D], F32) as xg, \
             nc.sbuf_tensor(f"xsc{tag}", [128, GRPl * D], F32) as xsc, \
             nc.sbuf_tensor(f"xlnTh{tag}", [128, DCH * TGl], BF16) as xlnT_h, \
             nc.sbuf_tensor(f"xlnTl{tag}", [128, DCH * TGl], BF16) as xlnT_l, \
             nc.sbuf_tensor(f"gbuf{tag}", [128, 2 * TGl], F32) as gbuf, \
             nc.sbuf_tensor(f"gwh{tag}", [128, HCH * TGl], BF16) as gw_h, \
             nc.sbuf_tensor(f"gwl{tag}", [128, HCH * TGl], BF16) as gw_l, \
             nc.sbuf_tensor(f"g2s{tag}", [128, TGl], F32) as g2s, \
             nc.sbuf_tensor(f"strow{tag}", [1, 2 * TGl], F32) as strow:
            bank_free = {0: 0, 1: 0}
            slot_ve = [0, 0]
            bank5_free = 0
            last_hstat = 0
            for g in range(NG):
                ch0 = g * GRPl
                sync.wait_ge(VE.h, VE.n)
                sync.wait_ge(LD.h, LD.n)
                n0 = LD.inc(sync.dma_start(
                    out=xg[:, :].rearrange("p (a c) -> p a c", a=GRPl), in_=x_hbm[ch0 * 128:(ch0 + GRPl) * 128, :].rearrange(
                        "(a p) c -> p a c", p=128)), 16)
                ve.wait_ge(LD.h, n0)
                for i in range(GRPl):
                    ch = ch0 + i
                    ts_(xsc[:, i * D:(i + 1) * D], xg[:, i * D:(i + 1) * D], 0.0, None,
                        Alu.add, Alu.add, accum_out=s_mu[:, ch:ch + 1])
                    stt_(xsc[:, i * D:(i + 1) * D], xg[:, i * D:(i + 1) * D], 0.0,
                         xg[:, i * D:(i + 1) * D], Alu.add, Alu.mult,
                         accum_out=s_rs[:, ch:ch + 1])
                ts_(s_mu[:, ch0:ch0 + GRPl], s_mu[:, ch0:ch0 + GRPl], 1.0 / D, None, Alu.mult)
                ts_(s_rs[:, ch0:ch0 + GRPl], s_rs[:, ch0:ch0 + GRPl], 1.0 / D, None, Alu.mult)
                tt_(vt1[:, 0:GRPl], s_mu[:, ch0:ch0 + GRPl], s_mu[:, ch0:ch0 + GRPl], Alu.mult)
                tt_(s_rs[:, ch0:ch0 + GRPl], s_rs[:, ch0:ch0 + GRPl], vt1[:, 0:GRPl], Alu.subtract)
                ts_(s_rs[:, ch0:ch0 + GRPl], s_rs[:, ch0:ch0 + GRPl], LN_EPS, None, Alu.add)
                emit_rsqrt(s_rs[:, ch0:ch0 + GRPl], s_rs[:, ch0:ch0 + GRPl],
                           vt1[:, 0:GRPl], vt2[:, 0:GRPl])
                for i in range(GRPl):
                    ch = ch0 + i
                    vn = VE.inc(ts_(xsc[:, i * D:(i + 1) * D], xg[:, i * D:(i + 1) * D],
                                    s_mu[:, ch:ch + 1], s_rs[:, ch:ch + 1],
                                    Alu.subtract, Alu.mult))
                    te.wait_ge(VE.h, vn)
                    for dq in range(0, DCH, 4):
                        for t4 in range(4):
                            dc = dq + t4
                            PE.inc(trans(pb[4][:, t4 * 128:(t4 + 1) * 128],
                                         xsc[:, i * D + dc * 128:i * D + (dc + 1) * 128],
                                         first=(t4 == 0), last=(t4 == 3)))
                        ve.wait_ge(PE.h, PE.n)
                        vn = None
                        for t4 in range(4):
                            dc = dq + t4
                            col = dc * TGl + i * 128
                            ve.tensor_copy(xlnT_h[:, col:col + 128],
                                           pb[4][:, t4 * 128:(t4 + 1) * 128])
                            vn = VE.inc(tt_(xlnT_l[:, col:col + 128],
                                            pb[4][:, t4 * 128:(t4 + 1) * 128],
                                            xlnT_h[:, col:col + 128], Alu.subtract))
                        te.wait_ge(VE.h, vn)
                te.wait_ge(VE.h, bank5_free)
                for hc in range(HCH):
                    sl = hc % 2
                    bank = pb[sl]
                    te.wait_ge(AC.h, bank_free[sl])
                    for p, srcx in ((0, xlnT_h), (1, xlnT_l)):
                        for dc in range(DCH):
                            PE.inc(te.matmul(
                                bank[:, 0:TGl],
                                wfcq_sb[:, (dc * HCH + hc) * 128:(dc * HCH + hc + 1) * 128],
                                srcx[:, dc * TGl:(dc + 1) * TGl],
                                start=(p == 0 and dc == 0),
                                stop=(p == 1 and dc == DCH - 1)))
                    ac.wait_ge(PE.h, PE.n)
                    ac.wait_ge(VE.h, slot_ve[sl])
                    AC.inc(ac.activation(gbuf[:, sl * TGl:(sl + 1) * TGl], bank[:, 0:TGl],
                                         AF.Gelu))
                    ac.wait_ge(PE.h, last_hstat)
                    an = AC.inc(ac.activation(g2s[:, 0:TGl], gbuf[:, sl * TGl:(sl + 1) * TGl],
                                              AF.Square))
                    bank_free[sl] = an
                    te.wait_ge(AC.h, an)
                    PE.inc(te.matmul(pb[5][0:1, 0:TGl], ones_col[:, :],
                                     gbuf[:, sl * TGl:(sl + 1) * TGl],
                                     start=(hc == 0), stop=(hc == HCH - 1),
                                     skip_group_check=True))
                    last_hstat = PE.inc(te.matmul(pb[5][0:1, TGl:2 * TGl], ones_col[:, :],
                                                  g2s[:, 0:TGl],
                                                  start=False, stop=(hc == HCH - 1),
                                                  skip_group_check=True))
                    ve.wait_ge(AC.h, an - 1)
                    ts_(gw_h[:, hc * TGl:(hc + 1) * TGl], gbuf[:, sl * TGl:(sl + 1) * TGl],
                        lnw_sb[:, hc:hc + 1], None, Alu.mult)
                    slot_ve[sl] = VE.inc(stt_(gw_l[:, hc * TGl:(hc + 1) * TGl],
                                              gbuf[:, sl * TGl:(sl + 1) * TGl],
                                              lnw_sb[:, hc:hc + 1],
                                              gw_h[:, hc * TGl:(hc + 1) * TGl],
                                              Alu.mult, Alu.subtract))
                ve.wait_ge(PE.h, last_hstat)
                vn = VE.inc(ve.tensor_copy(strow[0:1, 0:2 * TGl], pb[5][0:1, 0:2 * TGl]))
                bank5_free = vn
                te.wait_ge(VE.h, vn)
                for i in range(GRPl):
                    ch = ch0 + i
                    PE.inc(trans(pb[6][:, 0:1], strow[0:1, i * 128:(i + 1) * 128],
                                 first=True, last=False))
                    pn = PE.inc(trans(pb[6][:, 1:2], strow[0:1, TGl + i * 128:TGl + (i + 1) * 128],
                                      first=False, last=True))
                    ve.wait_ge(PE.h, pn)
                    ts_(s_hmu[:, ch:ch + 1], pb[6][:, 0:1], 1.0 / H, None, Alu.mult)
                    vn = VE.inc(ts_(s_hrs[:, ch:ch + 1], pb[6][:, 1:2], 1.0 / H, None, Alu.mult))
                    te.wait_ge(VE.h, vn)
                for i in range(GRPl):
                    ch = ch0 + i
                    tt_(vt1[:, 0:1], s_hmu[:, ch:ch + 1], s_hmu[:, ch:ch + 1], Alu.mult)
                    tt_(s_hrs[:, ch:ch + 1], s_hrs[:, ch:ch + 1], vt1[:, 0:1], Alu.subtract)
                    ts_(s_hrs[:, ch:ch + 1], s_hrs[:, ch:ch + 1], LN_EPS, None, Alu.add)
                emit_rsqrt(s_hrs[:, ch0:ch0 + GRPl], s_hrs[:, ch0:ch0 + GRPl],
                           vt1[:, 0:GRPl], vt2[:, 0:GRPl])
                ts_(vt1[:, 0:GRPl], s_hrs[:, ch0:ch0 + GRPl], 0.5, None, Alu.mult)
                tt_(vt2[:, 0:GRPl], s_hmu[:, ch0:ch0 + GRPl], s_hrs[:, ch0:ch0 + GRPl], Alu.mult)
                VE.inc(ts_(vt2[:, 0:GRPl], vt2[:, 0:GRPl], -1.0, None, Alu.mult))
                for i in range(GRPl):
                    for dh in range(2):
                        bank = pb[2 + dh]
                        for p, srcx in ((0, gw_h), (1, gw_l)):
                            for hc in range(HCH):
                                PE.inc(te.matmul(
                                    bank[:, 0:512],
                                    srcx[:, hc * TGl + i * 128:hc * TGl + (i + 1) * 128],
                                    wprjq_sb[:, hc * D + dh * 512:hc * D + (dh + 1) * 512],
                                    start=(p == 0 and hc == 0),
                                    stop=(p == 1 and hc == HCH - 1)))
                        ve.wait_ge(PE.h, PE.n)
                        xs = xg[:, i * D + dh * 512:i * D + (dh + 1) * 512]
                        stt_(xs, bank[:, 0:512], vt1[:, i:i + 1], xs, Alu.mult, Alu.add)
                        stt_(xs, wW_b[:, dh * 512:(dh + 1) * 512], vt2[:, i:i + 1], xs,
                             Alu.mult, Alu.add)
                        vn = VE.inc(tt_(xs, xs, bW_b[:, dh * 512:(dh + 1) * 512], Alu.add))
                        te.wait_ge(VE.h, vn)
                sync.wait_ge(VE.h, VE.n)
                LD.inc(sync.dma_start(
                    out=x_hbm[ch0 * 128:(ch0 + GRPl) * 128, :].rearrange(
                        "(a p) c -> p a c", p=128), in_=xg[:, :].rearrange("p (a c) -> p a c", a=GRPl)), 16)
                if write_y:
                    LD.inc(sync.dma_start(
                        out=y_out[ch0 * 128:(ch0 + GRPl) * 128, :].rearrange(
                            "(a p) c -> p a c", p=128), in_=xg[:, :].rearrange("p (a c) -> p a c", a=GRPl)), 16)
        barrier()


    ve.memset(ones_col[:, :], 1.0)
    ve.memset(ones_row[:, :], 1.0)
    ve.memset(half_row[:, :], 0.5)
    with nc.sbuf_tensor("iota_p", [128, 128], I32) as iota_p, \
         nc.sbuf_tensor("iota_f", [128, 128], I32) as iota_f:
        gp.iota(iota_p[:, :], [[0, 128]], base=0, channel_multiplier=1).then_inc(GP.h, 1); GP.n += 1
        gp.iota(iota_f[:, :], [[1, 128]], base=0, channel_multiplier=0).then_inc(GP.h, 1); GP.n += 1
        gp.iota(miota[:, :], [[1, 2 * BLK + 128]], base=-BLK, channel_multiplier=-1,
                allow_small_or_imprecise_dtypes=True).then_inc(GP.h, 1); GP.n += 1
        ve.wait_ge(GP.h, 2)
        VE.inc(tt_(ident[:, :], iota_p[:, :], iota_f[:, :], Alu.is_equal))

        for dst, src in ((s_dneg, dneg_d), (s_qidx, qidx_d), (s_vidx, vidx_d),
                         (s_kidx, kidx_d), (s_aidx, aidx_d), (s_mv, mvals_d),
                         (lnw_sb, lnw_d), (lnb_sb, lnb_d)):
            LD.inc(sync.dma_start(out=dst[:, :], in_=src), 16)
        ld_tables = LD.n

        with nc.sbuf_tensor("s_brow", [1, D], F32) as s_brow:
            for bd, bb in ((bk_d, bk_b), (bq_d, bq_b), (bv_d, bv_b), (bp_d, bp_b)):
                n0 = LD.inc(sync.dma_start(out=s_brow[:, :], in_=bd), 16)
                te.wait_ge(LD.h, n0)
                te.wait_ge(VE.h, VE.n)
                for hh in range(2):
                    pn = PE.inc(te.matmul(pb[6][:, 0:512], ones_row[:, :],
                                          s_brow[0:1, hh * 512:(hh + 1) * 512],
                                          start=True, stop=True))
                    ve.wait_ge(PE.h, pn)
                    vn = VE.inc(ve.tensor_copy(bb[:, hh * 512:(hh + 1) * 512], pb[6][:, 0:512]))
                    te.wait_ge(VE.h, vn)
                sync.wait_ge(VE.h, VE.n)

        ve.wait_ge(LD.h, LD.n)
        ve.memset(rot_c[:, :], 1.0)
        ve.memset(rot_s[:, :], 0.0)
        for j in range(14):
            cj = float(np.cos(np.float64(2 ** j))); sj = float(np.sin(np.float64(2 ** j)))
            ts_(bt_i[:, :], s_mv[:, :], j, None, Alu.arith_shift_right)
            ts_(bt_i[:, :], bt_i[:, :], 1, None, Alu.bitwise_and)
            ts_(bt_f[:, :], bt_i[:, :], -1.0, None, Alu.mult)
            ts_(vt1[:, :], rot_c[:, :], cj, None, Alu.mult)
            stt_(vt1[:, :], rot_s[:, :], -sj, vt1[:, :], Alu.mult, Alu.add)
            ts_(vt2[:, :], rot_c[:, :], sj, None, Alu.mult)
            stt_(vt2[:, :], rot_s[:, :], cj, vt2[:, :], Alu.mult, Alu.add)
            tt_(vt3[:, :], vt1[:, :], rot_c[:, :], Alu.subtract)
            tt_(vt3[:, :], vt3[:, :], bt_f[:, :], Alu.mult)
            tt_(rot_c[:, :], rot_c[:, :], vt3[:, :], Alu.subtract)
            tt_(vt3[:, :], vt2[:, :], rot_s[:, :], Alu.subtract)
            tt_(vt3[:, :], vt3[:, :], bt_f[:, :], Alu.mult)
            tt_(rot_s[:, :], rot_s[:, :], vt3[:, :], Alu.subtract)
        ve.tensor_copy(stepc[0][:, :], rot_c[:, :])
        ve.tensor_copy(steps[0][:, :], rot_s[:, :])
        for j in range(1, 9):
            tt_(vt1[:, :], stepc[j - 1][:, :], stepc[j - 1][:, :], Alu.mult)
            tt_(vt2[:, :], steps[j - 1][:, :], steps[j - 1][:, :], Alu.mult)
            tt_(stepc[j][:, :], vt1[:, :], vt2[:, :], Alu.subtract)
            tt_(vt1[:, :], stepc[j - 1][:, :], steps[j - 1][:, :], Alu.mult)
            ts_(steps[j][:, :], vt1[:, :], 2.0, None, Alu.mult)
        VE.inc()

    try:
        with nc.sbuf_tensor("wfcq", [128, DCH * HCH * 128], FP8) as wfcq, \
             nc.sbuf_tensor("wprjq", [128, HCH * D], BF16) as wprjq:
            with nc.sbuf_tensor("wpiece", [128, H], F32) as wpiece, \
                 nc.sbuf_tensor("wabs", [128, 1024], F32) as wabs, \
                 nc.sbuf_tensor("asum", [128, 64], F32) as asum, \
                 nc.sbuf_tensor("qtile", [128, 512], F32) as qtile, \
                 nc.sbuf_tensor("qpos", [128, 512], F32) as qpos, \
                 nc.sbuf_tensor("lnwb4", [128, 4 * HCH], BF16) as lnwb4, \
                 nc.sbuf_tensor("strow2", [1, 1024], F32) as strow2:

              def gamma_pass(w_dram, nrow, tau_col):
                  ncol = (H * D) // nrow
                  ac.wait_ge(VE.h, VE.n)
                  sync.wait_ge(VE.h, VE.n)
                  acc_cols = 0
                  for rc in range(nrow // 128):
                      n0 = LD.inc(sync.dma_start(out=wpiece[:, 0:ncol],
                                                 in_=w_dram[rc * 128:(rc + 1) * 128, :]), 16)
                      ac.wait_ge(LD.h, n0)
                      an = None
                      for gcol in range(ncol // 1024):
                          an = AC.inc(ac.activation(wabs[:, 0:1024],
                                                    wpiece[:, gcol * 1024:(gcol + 1) * 1024], AF.Abs,
                                                    accum_out=asum[:, acc_cols:acc_cols + 1]))
                          acc_cols += 1
                      sync.wait_ge(AC.h, an)
                  ve.wait_ge(AC.h, AC.n)
                  vn = VE.inc(ve.tensor_reduce(svec[:, 0:1], asum[:, 0:acc_cols],
                                               axis=mybir.AxisListType.X, op=Alu.add))
                  te.wait_ge(VE.h, vn)
                  pn = PE.inc(te.matmul(pb[6][0:1, 0:1], ones_col[:, :], svec[:, 0:1],
                                        start=True, stop=True))
                  ve.wait_ge(PE.h, pn)
                  vn = VE.inc(ve.tensor_copy(svec[0:1, 2:3], pb[6][0:1, 0:1]))
                  te.wait_ge(VE.h, vn)
                  pn = PE.inc(te.matmul(pb[6][:, 0:1], ones_row[:, :], svec[0:1, 2:3],
                                        start=True, stop=True))
                  ve.wait_ge(PE.h, pn)
                  ts_(s_tau[:, tau_col:tau_col + 1], pb[6][:, 0:1], 1.0 / (H * D), QEPS,
                      Alu.mult, Alu.add)
                  ts_(s_tau[:, tau_col:tau_col + 1], s_tau[:, tau_col:tau_col + 1], 0.5, None, Alu.mult)
                  VE.inc(ts_(s_tau[:, 2 + tau_col:3 + tau_col], s_tau[:, tau_col:tau_col + 1],
                             -1.0, None, Alu.mult))
                  te.wait_ge(VE.h, VE.n)

              gamma_pass(Wfc_d, H, 0)
              gamma_pass(Wproj_d, D, 1)

              def quant_transpose(w_dram, nrow, tau_col, dst_sb, dst_col_of):
                  ncol = (H * D) // nrow
                  for rc in range(nrow // 128):
                      sync.wait_ge(AC.h, AC.n)
                      sync.wait_ge(VE.h, VE.n)
                      n0 = LD.inc(sync.dma_start(out=wpiece[:, 0:ncol],
                                                 in_=w_dram[rc * 128:(rc + 1) * 128, :]), 16)
                      te.wait_ge(LD.h, n0)
                      for cc4 in range(0, ncol // 128, 4):
                          te.wait_ge(VE.h, VE.n)
                          for i in range(4):
                              cc = cc4 + i
                              PE.inc(trans(pb[4][:, i * 128:(i + 1) * 128],
                                           wpiece[:, cc * 128:(cc + 1) * 128],
                                           first=(i == 0), last=(i == 3)))
                          ve.wait_ge(PE.h, PE.n)
                          ts_(qpos[:, 0:512], pb[4][:, 0:512], s_tau[:, tau_col:tau_col + 1],
                              None, Alu.is_ge)
                          ts_(qtile[:, 0:512], pb[4][:, 0:512], s_tau[:, 2 + tau_col:3 + tau_col],
                              None, Alu.is_le)
                          for i in range(4):
                              cc = cc4 + i
                              dcol = dst_col_of(rc, cc)
                              VE.inc(tt_(dst_sb[:, dcol:dcol + 128], qpos[:, i * 128:(i + 1) * 128],
                                         qtile[:, i * 128:(i + 1) * 128], Alu.subtract))

              quant_transpose(Wfc_d, H, 0, wfcq, lambda rc, cc: (cc * HCH + rc) * 128)
              quant_transpose(Wproj_d, D, 1, wprjq, lambda rc, cc: cc * D + rc * 128)

              ve.wait_ge(LD.h, LD.n)
              for hc in range(HCH):
                  c0 = 4 * hc
                  ts_(lnwb4[:, c0:c0 + 1], lnw_sb[:, hc:hc + 1], 0.0, None, Alu.add)
                  stt_(lnwb4[:, c0 + 1:c0 + 2], lnw_sb[:, hc:hc + 1], -1.0,
                       lnwb4[:, c0:c0 + 1], Alu.mult, Alu.add)
                  ts_(lnwb4[:, c0 + 1:c0 + 2], lnwb4[:, c0 + 1:c0 + 2], -1.0, None, Alu.mult)
                  ts_(lnwb4[:, c0 + 2:c0 + 3], lnb_sb[:, hc:hc + 1], 0.0, None, Alu.add)
                  stt_(lnwb4[:, c0 + 3:c0 + 4], lnb_sb[:, hc:hc + 1], -1.0,
                       lnwb4[:, c0 + 2:c0 + 3], Alu.mult, Alu.add)
                  ts_(lnwb4[:, c0 + 3:c0 + 4], lnwb4[:, c0 + 3:c0 + 4], -1.0, None, Alu.mult)
              vn = VE.inc()
              te.wait_ge(VE.h, vn)
              for dh in range(2):
                  for which, dst in ((0, wW_b), (1, bW_b)):
                      for p2 in range(2):
                          col = which * 2 + p2
                          for hc in range(HCH):
                              PE.inc(te.matmul(pb[6][0:1, 0:512],
                                               lnwb4[:, 4 * hc + col:4 * hc + col + 1],
                                               wprjq[:, hc * D + dh * 512:hc * D + (dh + 1) * 512],
                                               start=(p2 == 0 and hc == 0),
                                               stop=(p2 == 1 and hc == HCH - 1)))
                      ve.wait_ge(PE.h, PE.n)
                      vn = VE.inc(ve.tensor_copy(strow2[0:1, 0:512], pb[6][0:1, 0:512]))
                      te.wait_ge(VE.h, vn)
                      pn = PE.inc(te.matmul(pb[6][:, 0:512], half_row[:, :],
                                            strow2[0:1, 0:512], start=True, stop=True))
                      ve.wait_ge(PE.h, pn)
                      vn = VE.inc(ve.tensor_copy(dst[:, dh * 512:(dh + 1) * 512], pb[6][:, 0:512]))
                      te.wait_ge(VE.h, vn)

              sync.wait_ge(VE.h, VE.n)
              LD.inc(sync.dma_start(out=wfcq_hbm[:, :], in_=wfcq[:, :]), 16)
              LD.inc(sync.dma_start(out=wprjq_hbm[:, :], in_=wprjq[:, :]), 16)
              LD.inc(sync.dma_start(out=x_hbm[:, :], in_=x_in), 16)
              with nc.sbuf_tensor("zer", [128, ACCW], F32) as zer:
                  vn = VE.inc(ve.memset(zer[:, :], 0.0))
                  sync.wait_ge(VE.h, vn)
                  for i in range(TPC // 128):
                      LD.inc(sync.dma_start(out=acc_hbm[i * 128:(i + 1) * 128, :], in_=zer[:, :]), 16)
              barrier()

            if stop_after != "prep":
                ffn_pass(wfcq, wprjq, write_y=False, tag="a")
            if stop_after in ("prep", "ffn1"):
                with nc.sbuf_tensor("dmp", [128, GRP * D], F32) as dmp:
                    for g in range(NG):
                        n0 = LD.inc(sync.dma_start(
                            out=dmp[:, :].rearrange("p (a c) -> p a c", a=GRP),
                            in_=x_hbm[g * GRP * 128:(g + 1) * GRP * 128, :].rearrange(
                                "(a p) c -> p a c", p=128)), 16)
                        sync.wait_ge(LD.h, n0)
                        n1 = LD.inc(sync.dma_start(
                            out=y_out[g * GRP * 128:(g + 1) * GRP * 128, :].rearrange(
                                "(a p) c -> p a c", p=128),
                            in_=dmp[:, :].rearrange("p (a c) -> p a c", a=GRP)), 16)
                        sync.wait_ge(LD.h, n1)
                raise _Stop


        with nc.sbuf_tensor("w3", [128, 3 * DCH * D], F32) as w3, \
             nc.sbuf_tensor("xg2", [128, GRP * D], F32) as xg2, \
             nc.sbuf_tensor("xsc2", [128, GRP * D], F32) as xsc2, \
             nc.sbuf_tensor("ln2T", [128, DCH * TG], F32) as ln2T, \
             nc.sbuf_tensor("kqv", [128, 3 * D], F32) as kqv, \
             nc.sbuf_tensor("krot", [128, 2 * D], F32) as krot, \
             nc.sbuf_tensor("tabc", [128, 512], F32) as tabc, \
             nc.sbuf_tensor("tabs", [128, 512], F32) as tabs, \
             nc.sbuf_tensor("rtmp", [128, 512], F32) as rtmp, \
             nc.sbuf_tensor("rtmp2", [128, 512], F32) as rtmp2, \
             nc.sbuf_tensor("tpose", [128, 8 * 128], F32) as tpose:
            for wi, wd in enumerate((Wk_d, Wq_d, Wv_d)):
                LD.inc(sync.dma_start(
                    out=w3[:, wi * DCH * D:(wi + 1) * DCH * D].rearrange("p (a c) -> p a c", a=DCH),
                    in_=wd.rearrange("(a p) c -> p a c", p=128)), 16)
            ld_w3 = LD.n
            te.wait_ge(LD.h, ld_w3)
            for g in range(NG):
                ch0 = g * GRP
                sync.wait_ge(VE.h, VE.n)
                sync.wait_ge(LD.h, LD.n)
                n0 = LD.inc(sync.dma_start(
                    out=xg2[:, :].rearrange("p (a c) -> p a c", a=GRP), in_=x_hbm[ch0 * 128:(ch0 + GRP) * 128, :].rearrange(
                        "(a p) c -> p a c", p=128)), 16)
                ve.wait_ge(LD.h, n0)
                for i in range(GRP):
                    ch = ch0 + i
                    ts_(xsc2[:, i * D:(i + 1) * D], xg2[:, i * D:(i + 1) * D], 0.0, None,
                        Alu.add, Alu.add, accum_out=s_mu[:, ch:ch + 1])
                    stt_(xsc2[:, i * D:(i + 1) * D], xg2[:, i * D:(i + 1) * D], 0.0,
                         xg2[:, i * D:(i + 1) * D], Alu.add, Alu.mult,
                         accum_out=s_rs[:, ch:ch + 1])
                ts_(s_mu[:, ch0:ch0 + GRP], s_mu[:, ch0:ch0 + GRP], 1.0 / D, None, Alu.mult)
                ts_(s_rs[:, ch0:ch0 + GRP], s_rs[:, ch0:ch0 + GRP], 1.0 / D, None, Alu.mult)
                tt_(vt1[:, 0:GRP], s_mu[:, ch0:ch0 + GRP], s_mu[:, ch0:ch0 + GRP], Alu.mult)
                tt_(s_rs[:, ch0:ch0 + GRP], s_rs[:, ch0:ch0 + GRP], vt1[:, 0:GRP], Alu.subtract)
                ts_(s_rs[:, ch0:ch0 + GRP], s_rs[:, ch0:ch0 + GRP], LN_EPS, None, Alu.add)
                emit_rsqrt(s_rs[:, ch0:ch0 + GRP], s_rs[:, ch0:ch0 + GRP],
                           vt1[:, 0:GRP], vt2[:, 0:GRP])
                for i in range(GRP):
                    ch = ch0 + i
                    blk_i = ch // NIC
                    tok0 = (ch % NIC) * 128
                    vn = VE.inc(ts_(xsc2[:, i * D:(i + 1) * D], xg2[:, i * D:(i + 1) * D],
                                    s_mu[:, ch:ch + 1], s_rs[:, ch:ch + 1],
                                    Alu.subtract, Alu.mult))
                    te.wait_ge(VE.h, vn)
                    for dq in range(0, DCH, 4):
                        for t4 in range(4):
                            dc = dq + t4
                            PE.inc(trans(pb[4][:, t4 * 128:(t4 + 1) * 128],
                                         xsc2[:, i * D + dc * 128:i * D + (dc + 1) * 128],
                                         first=(t4 == 0), last=(t4 == 3)))
                        ve.wait_ge(PE.h, PE.n)
                        vn = None
                        for t4 in range(4):
                            dc = dq + t4
                            vn = VE.inc(ve.tensor_copy(ln2T[:, dc * 128:(dc + 1) * 128],
                                                       pb[4][:, t4 * 128:(t4 + 1) * 128]))
                        te.wait_ge(VE.h, vn)
                    ve.wait_ge(LD.h, LD.n)
                    for wi in range(3):
                        for dh in range(2):
                            bank = pb[(wi * 2 + dh) % 4]
                            for dc in range(DCH):
                                PE.inc(te.matmul(bank[:, 0:512],
                                                 ln2T[:, dc * 128:(dc + 1) * 128],
                                                 w3[:, wi * DCH * D + dc * D + dh * 512:
                                                    wi * DCH * D + dc * D + (dh + 1) * 512],
                                                 start=(dc == 0), stop=(dc == DCH - 1)))
                            pn = PE.n
                            ve.wait_ge(PE.h, pn)
                            dst = kqv[:, wi * D + dh * 512:wi * D + (dh + 1) * 512]
                            if wi == 0:
                                vn = VE.inc(tt_(dst, bank[:, 0:512],
                                                bk_b[:, dh * 512:(dh + 1) * 512], Alu.add))
                            elif wi == 1:
                                vn = VE.inc(tt_(dst, bank[:, 0:512],
                                                bq_b[:, dh * 512:(dh + 1) * 512], Alu.add))
                            else:
                                vn = VE.inc(ve.tensor_copy(dst, bank[:, 0:512]))
                            te.wait_ge(VE.h, vn)
                    ve.memset(tabc[:, 0:1], 1.0)
                    ve.memset(tabs[:, 0:1], 0.0)
                    for r in range(9):
                        n = 1 << r
                        sc = stepc[r][:, ch:ch + 1]; ss = steps[r][:, ch:ch + 1]
                        ts_(rtmp[:, 0:n], tabc[:, 0:n], sc, None, Alu.mult)
                        ts_(rtmp2[:, 0:n], tabs[:, 0:n], ss, None, Alu.mult)
                        tt_(tabc[:, n:2 * n], rtmp[:, 0:n], rtmp2[:, 0:n], Alu.subtract)
                        ts_(rtmp[:, 0:n], tabc[:, 0:n], ss, None, Alu.mult)
                        stt_(tabs[:, n:2 * n], tabs[:, 0:n], sc, rtmp[:, 0:n], Alu.mult, Alu.add)
                    for t_i in range(2):
                        src = kqv[:, t_i * D:(t_i + 1) * D]
                        dst = krot[:, t_i * D:(t_i + 1) * D]
                        z0 = src[:, 0:D:2]; z1 = src[:, 1:D:2]
                        r0 = dst[:, 0:D:2]; r1 = dst[:, 1:D:2]
                        tt_(rtmp[:, 0:512], tabc[:, :], z0, Alu.mult)
                        tt_(rtmp2[:, 0:512], tabs[:, :], z1, Alu.mult)
                        tt_(r0, rtmp[:, 0:512], rtmp2[:, 0:512], Alu.add)
                        tt_(rtmp[:, 0:512], tabs[:, :], z0, Alu.mult)
                        tt_(rtmp2[:, 0:512], tabc[:, :], z1, Alu.mult)
                        VE.inc(tt_(r1, rtmp2[:, 0:512], rtmp[:, 0:512], Alu.subtract))
                    for t_i, dst_hbm in ((0, kT_hbm), (1, qb_in)):
                        te.wait_ge(VE.h, VE.n)
                        ve.wait_ge(LD.h, LD.n)
                        for dq in range(0, DCH, 4):
                            sync.wait_ge(VE.h, VE.n)
                            for t4 in range(4):
                                dc = dq + t4
                                PE.inc(trans(pb[4][:, t4 * 128:(t4 + 1) * 128],
                                             krot[:, t_i * D + dc * 128:t_i * D + (dc + 1) * 128],
                                             first=(t4 == 0), last=(t4 == 3)))
                            ve.wait_ge(PE.h, PE.n)
                            vn = None
                            for t4 in range(4):
                                vn = VE.inc(ve.tensor_copy(tpose[:, (dq + t4) * 128:(dq + t4 + 1) * 128],
                                                           pb[4][:, t4 * 128:(t4 + 1) * 128]))
                            te.wait_ge(VE.h, vn)
                        sync.wait_ge(VE.h, VE.n)
                        for dc in range(DCH):
                            LD.inc(sync.dma_start(
                                out=dst_hbm[blk_i * D + dc * 128:blk_i * D + (dc + 1) * 128,
                                            tok0:tok0 + 128],
                                in_=tpose[:, dc * 128:(dc + 1) * 128]), 16)
                    vrow0 = blk_i * BLK + tok0
                    LD.inc(sync.dma_start(out=vb_in[vrow0:vrow0 + 128, :],
                                          in_=kqv[:, 2 * D:3 * D]), 16)
            barrier()

        if stop_after == "qkv":
            raise _Stop
        gp.wait_ge(LD.h, LD.n)
        gp.collective_compute("AllGather", Alu.bypass,
                              replica_groups=[list(range(NCORES))],
                              ins=[qb_in.ap().opt()], outs=[qb_out.ap().opt()]
                              ).then_inc(CC.h, 1)
        CC.n += 1
        gp.collective_compute("AllGather", Alu.bypass,
                              replica_groups=[list(range(NCORES))],
                              ins=[vb_in.ap().opt()], outs=[vb_out.ap().opt()]
                              ).then_inc(CC.h, 1)
        CC.n += 1
        barrier()

        with nc.sbuf_tensor("kTb", [128, DCH * BLK], F32) as kTb, \
             nc.sbuf_tensor("qTb", [128, DCH * BLK], F32) as qTb, \
             nc.sbuf_tensor("vb", [128, NIC * D], F32) as vb, \
             nc.sbuf_tensor("wei", [128, NJC * BLK], F32) as wei, \
             nc.sbuf_tensor("mtile", [128, ICOL], F32) as mtile, \
             nc.sbuf_tensor("stg0", [128, NIC * ACCW], F32) as stg0:
            stgs = [stg0]
            for sb in stgs:
                for ic in range(NIC):
                    ve.memset(sb[:, ic * ACCW + 1025:(ic + 1) * ACCW], 0.0)
            VE.inc()
            gp.wait_ge(VE.h, VE.n)
            kq_loaded = [0] * 17
            v_loaded = [0] * 17
            scores_done = [0] * 17
            wv_done = [0] * 17
            stg_free = [0]
            sp_done = [0] * 17
            for cc in range(DCH):
                KQ.inc(gp.indirect_dma_start(
                    kTb[:, cc * BLK:(cc + 1) * BLK], None, kT_hbm[:, :],
                    IndirectOffsetOnAxis(ap=s_kidx[:, 0 * DCH + cc:0 * DCH + cc + 1], axis=0)), 16)
                KQ.inc(gp.indirect_dma_start(
                    qTb[:, cc * BLK:(cc + 1) * BLK], None, qb_out[:, :],
                    IndirectOffsetOnAxis(ap=s_qidx[:, 0 * DCH + cc:0 * DCH + cc + 1], axis=0)), 16)
            kq_loaded[0] = KQ.n
            for tc in range(NIC):
                VB.inc(gp.indirect_dma_start(
                    vb[:, tc * D:(tc + 1) * D], None, vb_out[:, :],
                    IndirectOffsetOnAxis(ap=s_vidx[:, 0 * NIC + tc:0 * NIC + tc + 1], axis=0)), 16)
            v_loaded[0] = VB.n

            bankf = {0: 0, 1: 0, 2: 0, 3: 0}
            bank5_f = 0
            wv_pe_prev = 0
            for p in range(17):
                te.wait_ge(KQ.h, kq_loaded[p])
                ve.wait_ge(PE.h, wv_pe_prev)
                spn = None
                for jj in range(NJC):
                    for ih in range(NIT):
                        b = (jj * NIT + ih) % 2
                        bank = pb[b]
                        te.wait_ge(VE.h, bankf[b])
                        for cc in range(DCH):
                            PE.inc(te.matmul(bank[:, 0:ICOL],
                                             qTb[:, cc * BLK + jj * 128:cc * BLK + (jj + 1) * 128],
                                             kTb[:, cc * BLK + ih * ICOL:cc * BLK + (ih + 1) * ICOL],
                                             start=(cc == 0), stop=(cc == DCH - 1)))
                        pn = PE.n
                        off = jj * 128 - ih * ICOL
                        y0 = BLK - off
                        ve.wait_ge(PE.h, pn)
                        ts_(mtile[:, 0:ICOL], miota[:, y0:y0 + ICOL], s_dneg[:, p:p + 1],
                            NEG_BIG, Alu.is_lt, Alu.mult)
                        wcol = jj * BLK + ih * ICOL
                        vn = VE.inc(tt_(wei[:, wcol:wcol + ICOL], bank[:, 0:ICOL],
                                        mtile[:, 0:ICOL], Alu.add))
                        ac.wait_ge(VE.h, vn)
                        ac.activation(wei[:, wcol:wcol + ICOL], wei[:, wcol:wcol + ICOL],
                                      AF.Exp, scale=INV_SQRT_C)
                        spn = AC.inc(ac.activation(wei[:, wcol:wcol + ICOL], wei[:, wcol:wcol + ICOL],
                                                   AF.Ln, bias=1.0, scale=1.0))
                        bankf[b] = vn
                sp_done[p] = spn
                scores_done[p] = PE.n
                if p + 1 < 17:
                    gp.wait_ge(PE.h, scores_done[p])
                    for cc in range(DCH):
                        KQ.inc(gp.indirect_dma_start(
                            kTb[:, cc * BLK:(cc + 1) * BLK], None, kT_hbm[:, :],
                            IndirectOffsetOnAxis(ap=s_kidx[:, (p + 1) * DCH + cc:(p + 1) * DCH + cc + 1],
                                                 axis=0)), 16)
                        KQ.inc(gp.indirect_dma_start(
                            qTb[:, cc * BLK:(cc + 1) * BLK], None, qb_out[:, :],
                            IndirectOffsetOnAxis(ap=s_qidx[:, (p + 1) * DCH + cc:(p + 1) * DCH + cc + 1],
                                                 axis=0)), 16)
                    kq_loaded[p + 1] = KQ.n
                stg = stgs[0]
                te.wait_ge(AC.h, sp_done[p])
                te.wait_ge(VB.h, v_loaded[p])
                ve.wait_ge(SC.h, stg_free[0])
                te.wait_ge(VE.h, bank5_f)
                first_R = True
                for ic in range(NIC):
                    for dh in range(2):
                        b = 2 + (ic * 2 + dh) % 2
                        bank = pb[b]
                        te.wait_ge(VE.h, bankf[b])
                        for jj in range(NJC):
                            PE.inc(te.matmul(bank[:, 0:512],
                                             wei[:, jj * BLK + ic * 128:jj * BLK + (ic + 1) * 128],
                                             vb[:, jj * D + dh * 512:jj * D + (dh + 1) * 512],
                                             start=(jj == 0), stop=(jj == NJC - 1)))
                            if dh == 0:
                                PE.inc(te.matmul(pb[5][:, ic:ic + 1],
                                                 wei[:, jj * BLK + ic * 128:jj * BLK + (ic + 1) * 128],
                                                 ones_col[:, :],
                                                 start=first_R, stop=False,
                                                 skip_group_check=True))
                                first_R = False
                        pn = PE.n
                        ve.wait_ge(PE.h, pn)
                        vn = VE.inc(ve.tensor_copy(
                            stg[:, ic * ACCW + dh * 512:ic * ACCW + (dh + 1) * 512],
                            bank[:, 0:512]))
                        bankf[b] = vn
                ve.wait_ge(PE.h, PE.n)
                vn = VE.inc(ve.tensor_copy(stg[:, 1024:(NIC - 1) * ACCW + 1025:ACCW],
                                           pb[5][:, 0:NIC]))
                bank5_f = vn
                wv_done[p] = VE.n
                if p + 1 < 17:
                    gp.wait_ge(VE.h, wv_done[p])
                    for tc in range(NIC):
                        VB.inc(gp.indirect_dma_start(
                            vb[:, tc * D:(tc + 1) * D], None, vb_out[:, :],
                            IndirectOffsetOnAxis(ap=s_vidx[:, (p + 1) * NIC + tc:(p + 1) * NIC + tc + 1],
                                                 axis=0)), 16)
                    v_loaded[p + 1] = VB.n
                gp.wait_ge(VE.h, wv_done[p])
                for ic in range(NIC):
                    SC.inc(gp.indirect_dma_start(
                        acc_hbm[:, :],
                        IndirectOffsetOnAxis(ap=s_aidx[:, p * NIC + ic:p * NIC + ic + 1], axis=0),
                        stg[:, ic * ACCW:(ic + 1) * ACCW], None,
                        compute_op=Alu.add), 16)
                    gp.wait_ge(SC.h, SC.n)
                stg_free[0] = SC.n
                wv_pe_prev = PE.n
            barrier()

        if stop_after == "attn":
            raise _Stop
        with nc.sbuf_tensor("wp", [128, DCH * D], F32) as wp, \
             nc.sbuf_tensor("accs", [128, ACCW], F32) as accs, \
             nc.sbuf_tensor("att", [128, D], F32) as att, \
             nc.sbuf_tensor("attT", [128, DCH * 128], F32) as attT, \
             nc.sbuf_tensor("xg3", [128, D], F32) as xg3:
            LD.inc(sync.dma_start(out=wp[:, :].rearrange("p (a c) -> p a c", a=DCH),
                                  in_=Wp_d.rearrange("(a p) c -> p a c", p=128)), 16)
            ld_wp = LD.n
            te.wait_ge(LD.h, ld_wp)
            for ch in range(NCH):
                sync.wait_ge(VE.h, VE.n)
                sync.wait_ge(LD.h, LD.n)
                n0 = LD.inc(sync.dma_start(out=accs[:, :],
                                           in_=acc_hbm[ch * 128:(ch + 1) * 128, :]), 16)
                n1 = LD.inc(sync.dma_start(out=xg3[:, :],
                                           in_=x_hbm[ch * 128:(ch + 1) * 128, :]), 16)
                ve.wait_ge(LD.h, n1)
                stt_(att[:, 0:512], bv_b[:, 0:512], accs[:, 1024:1025], accs[:, 0:512],
                     Alu.mult, Alu.add)
                vn = VE.inc(stt_(att[:, 512:1024], bv_b[:, 512:1024], accs[:, 1024:1025],
                                 accs[:, 512:1024], Alu.mult, Alu.add))
                te.wait_ge(VE.h, vn)
                for dq in range(0, DCH, 4):
                    for t4 in range(4):
                        dc = dq + t4
                        PE.inc(trans(pb[4][:, t4 * 128:(t4 + 1) * 128],
                                     att[:, dc * 128:(dc + 1) * 128],
                                     first=(t4 == 0), last=(t4 == 3)))
                    ve.wait_ge(PE.h, PE.n)
                    vn = None
                    for t4 in range(4):
                        vn = VE.inc(ve.tensor_copy(attT[:, (dq + t4) * 128:(dq + t4 + 1) * 128],
                                                   pb[4][:, t4 * 128:(t4 + 1) * 128]))
                    te.wait_ge(VE.h, vn)
                for dh in range(2):
                    bank = pb[dh]
                    for dc in range(DCH):
                        PE.inc(te.matmul(bank[:, 0:512], attT[:, dc * 128:(dc + 1) * 128],
                                         wp[:, dc * D + dh * 512:dc * D + (dh + 1) * 512],
                                         start=(dc == 0), stop=(dc == DCH - 1)))
                    pn = PE.n
                    ve.wait_ge(PE.h, pn)
                    xs = xg3[:, dh * 512:(dh + 1) * 512]
                    tt_(xs, xs, bank[:, 0:512], Alu.add)
                    vn = VE.inc(tt_(xs, xs, bp_b[:, dh * 512:(dh + 1) * 512], Alu.add))
                    te.wait_ge(VE.h, vn)
                sync.wait_ge(VE.h, VE.n)
                LD.inc(sync.dma_start(out=x_hbm[ch * 128:(ch + 1) * 128, :], in_=xg3[:, :]), 16)
            barrier()

        if stop_after == "pproj":
            raise _Stop
        with nc.sbuf_tensor("wfcq2", [128, DCH * HCH * 128], FP8) as wfcq2, \
             nc.sbuf_tensor("wprjq2", [128, HCH * D], BF16) as wprjq2:
            LD.inc(sync.dma_start(out=wfcq2[:, :], in_=wfcq_hbm[:, :]), 16)
            LD.inc(sync.dma_start(out=wprjq2[:, :], in_=wprjq_hbm[:, :]), 16)
            wld = LD.n
            te.wait_ge(LD.h, wld)
            ve.wait_ge(LD.h, wld)

            ffn_pass(wfcq2, wprjq2, write_y=True, tag="b")

    except _Stop:
        pass

    sync.wait_ge(LD.h, LD.n)
    sync.wait_ge(GD.h, GD.n)
    sync.wait_ge(SC.h, SC.n)
    try:
        octx.close()
    except AssertionError:
        pass
    return nc



def _pairs_for_core(c):
    bA, bB = c, NBLK - 1 - c
    return ([(J, 0, bA) for J in range(bA + 1)]
            + [(J, 1, bB) for J in range(bB + 1)])


def _core_tables(c, cfg):
    BLK, NIC = cfg.BLK, cfg.NIC
    pairs = _pairs_for_core(c)
    assert len(pairs) == 17
    pp = np.arange(128)
    qidx = np.zeros((128, 17 * DCH), np.int32)
    kidx = np.zeros((128, 17 * DCH), np.int32)
    vidx = np.zeros((128, 17 * NIC), np.int32)
    aidx = np.zeros((128, 17 * NIC), np.int32)
    dneg = np.zeros((128, 17), np.float32)
    for p, (J, s_I, I_blk) in enumerate(pairs):
        oJ = J if J < NCORES else NBLK - 1 - J
        sJ = 0 if J < NCORES else 1
        for cc in range(DCH):
            qidx[:, p * DCH + cc] = oJ * 2 * D + sJ * D + cc * 128 + pp
            kidx[:, p * DCH + cc] = s_I * D + cc * 128 + pp
        for tc in range(NIC):
            vidx[:, p * NIC + tc] = oJ * cfg.TPC + sJ * BLK + tc * 128 + pp
            aidx[:, p * NIC + tc] = s_I * BLK + tc * 128 + pp
        dneg[:, p] = -float((I_blk - J) * BLK)
    return qidx, kidx, vidx, aidx, dneg


def _shard_host(cfg, inputs):
    T = cfg.T
    x = np.ascontiguousarray(np.asarray(inputs["L0_attn_x"], np.float32).reshape(T, D))
    lnw = np.ascontiguousarray(np.asarray(inputs["L0_ffn_ln_w"], np.float32).reshape(HCH, 128).T)
    lnb = np.ascontiguousarray(np.asarray(inputs["L0_ffn_ln_b"], np.float32).reshape(HCH, 128).T)
    common = {
        "Wfc": np.ascontiguousarray(np.asarray(inputs["L0_ffn_W_fc"], np.float32)),
        "Wproj": np.ascontiguousarray(np.asarray(inputs["L0_ffn_W_proj"], np.float32)),
        "Wk": np.ascontiguousarray(np.asarray(inputs["L0_attn_Wk"], np.float32)),
        "Wq": np.ascontiguousarray(np.asarray(inputs["L0_attn_Wq"], np.float32)),
        "Wv": np.ascontiguousarray(np.asarray(inputs["L0_attn_Wv"], np.float32)),
        "Wp": np.ascontiguousarray(np.asarray(inputs["L0_attn_Wp"], np.float32)),
        "bk": np.ascontiguousarray(np.asarray(inputs["L0_attn_bk"], np.float32).reshape(1, D)),
        "bq": np.ascontiguousarray(np.asarray(inputs["L0_attn_bq"], np.float32).reshape(1, D)),
        "bv": np.ascontiguousarray(np.asarray(inputs["L0_attn_bv"], np.float32).reshape(1, D)),
        "bp": np.ascontiguousarray(np.asarray(inputs["L0_attn_bp"], np.float32).reshape(1, D)),
        "lnw_t": lnw, "lnb_t": lnb,
    }
    in_maps = []
    for c in range(NCORES):
        bA, bB = c, NBLK - 1 - c
        xs = np.concatenate([x[bA * cfg.BLK:(bA + 1) * cfg.BLK],
                             x[bB * cfg.BLK:(bB + 1) * cfg.BLK]], axis=0)
        qidx, kidx, vidx, aidx, dneg = _core_tables(c, cfg)
        toks = np.concatenate([np.arange(bA * cfg.BLK, (bA + 1) * cfg.BLK),
                               np.arange(bB * cfg.BLK, (bB + 1) * cfg.BLK)])
        m = (cfg.T - 1 - toks).astype(np.int32).reshape(cfg.NCH, 128).T
        im = dict(common)
        im["x_in"] = np.ascontiguousarray(xs)
        im["m_vals"] = np.ascontiguousarray(m)
        im["dneg"] = dneg
        im["qidx"] = qidx; im["kidx"] = kidx; im["vidx"] = vidx; im["aidx"] = aidx
        in_maps.append(im)
    return in_maps


def _gather_host(cfg, results):
    T = cfg.T
    y = np.zeros((T, D), np.float32)
    for c in range(NCORES):
        bA, bB = c, NBLK - 1 - c
        yc = results[c]["y"]
        y[bA * cfg.BLK:(bA + 1) * cfg.BLK] = yc[:cfg.BLK]
        y[bB * cfg.BLK:(bB + 1) * cfg.BLK] = yc[cfg.BLK:]
    return y.reshape(1, T, D)


def kernel(**inputs):
    T = int(np.asarray(inputs["L0_attn_x"]).shape[1])
    cfg = Cfg(T)
    nc = build_program(cfg)
    in_maps = _shard_host(cfg, inputs)
    res = run_bass_kernel_spmd(nc, in_maps, core_ids=list(range(NCORES)))
    return _gather_host(cfg, res.results)


# version 04
# speedup vs baseline: 1.1562x; 1.1562x over previous
import numpy as np
import ml_dtypes
import concourse.bass as bass
import concourse.mybir as mybir
from concourse.bass_utils import run_bass_kernel_spmd
from bass_rust import IndirectOffsetOnAxis
from contextlib import ExitStack

F32 = mybir.dt.float32
BF16 = mybir.dt.bfloat16
FP8 = mybir.dt.float8e4
I32 = mybir.dt.int32
I16 = mybir.dt.int16
AF = mybir.ActivationFunctionType
Alu = mybir.AluOpType

NCORES = 8
D = 1024
H = 4096
DCH = D // 128
HCH = H // 128
NBLK = 16
LN_EPS = 1e-5
QEPS = 1e-6
ACCW = 1088
INV_SQRT_C = 1.0 / 32.0
NEG_BIG = -1e30


class _Stop(Exception):
    pass


class Cfg:
    def __init__(self, T):
        self.T = T
        self.BLK = T // NBLK
        self.TPC = 2 * self.BLK
        self.NCH = self.TPC // 128
        self.NJC = self.BLK // 128
        self.ICOL = min(512, self.BLK)
        self.NIT = self.BLK // self.ICOL
        self.NIC = self.BLK // 128
        self.GRP = 2
        self.NG = self.NCH // self.GRP
        self.TG = 128 * self.GRP


class Sem:
    def __init__(self, nc, ctx, name):
        self.h = ctx.enter_context(nc.semaphore(name))
        self.n = 0

    def inc(self, instr, kk=1):
        instr.then_inc(self.h, kk)
        self.n += kk
        return self.n


class FenceSem(Sem):
    def __init__(self, nc, ctx, name, eng, wrap):
        super().__init__(nc, ctx, name)
        self._eng = eng
        self._wrap = wrap

    def inc(self, instr=None, kk=1):
        self._eng.drain()
        if self._wrap is not None:
            self._wrap._written.clear()
            self._wrap._read.clear()
        self._eng.nop().then_inc(self.h, kk)
        self.n += kk
        return self.n




class EngWrap:
    _OPS = ("tensor_scalar", "tensor_tensor", "scalar_tensor_tensor", "tensor_copy",
            "memset", "reciprocal", "tensor_reduce", "activation", "iota",
            "tensor_add", "tensor_sub", "tensor_mul", "memzero")

    def __init__(self, eng):
        self._e = eng
        self._written = set()
        self._read = set()

    def __getattr__(self, name):
        attr = getattr(self._e, name)
        if name not in self._OPS:
            return attr

        def call(*args, **kwargs):
            names_in = []
            names_out = []
            for i, a in enumerate(args):
                if hasattr(a, "tensor") and hasattr(a.tensor, "name"):
                    (names_out if i == 0 else names_in).append(a.tensor.name)
            for kk, a in kwargs.items():
                if hasattr(a, "tensor") and hasattr(a.tensor, "name"):
                    (names_out if kk == "accum_out" else names_in).append(a.tensor.name)
            if (any(n in self._written for n in names_in + names_out)
                    or any(n in self._read for n in names_out)):
                self._e.drain()
                self._written.clear()
                self._read.clear()
            r = attr(*args, **kwargs)
            self._written.update(names_out)
            self._read.update(names_in)
            return r

        return call


def build_program(cfg, stop_after="full"):
    nc = bass.Bass("TRN2", target_bir_lowering=False)
    BLK, TPC, NCH, GRP, TG, NG = cfg.BLK, cfg.TPC, cfg.NCH, cfg.GRP, cfg.TG, cfg.NG
    NJC, ICOL, NIT, NIC = cfg.NJC, cfg.ICOL, cfg.NIT, cfg.NIC

    def din(name, shape, dt=F32):
        return nc.dram_tensor(name, shape, dt, kind="ExternalInput").ap()

    x_in = din("x_in", [TPC, D])
    Wfc_d = din("Wfc", [H, D]); Wproj_d = din("Wproj", [D, H])
    Wk_d = din("Wk", [D, D]); Wq_d = din("Wq", [D, D]); Wv_d = din("Wv", [D, D]); Wp_d = din("Wp", [D, D])
    bk_d = din("bk", [1, D]); bq_d = din("bq", [1, D]); bv_d = din("bv", [1, D]); bp_d = din("bp", [1, D])
    lnw_d = din("lnw_t", [128, HCH]); lnb_d = din("lnb_t", [128, HCH])
    mvals_d = din("m_vals", [128, NCH], I32)
    dneg_d = din("dneg", [128, 17])
    qidx_d = din("qidx", [128, 17 * DCH * 2], I32)
    vidx_d = din("vidx", [128, 17 * NIC * 2], I32)
    kidx_d = din("kidx", [128, 17 * DCH * 2], I32)
    aidx_d = din("aidx", [128, 17 * NIC], I32)
    y_out = nc.dram_tensor("y", [TPC, D], F32, kind="ExternalOutput").ap()

    x_hbm = nc.dram_tensor("x_hbm", [TPC, D], F32, kind="Internal")
    kT_hbm = nc.dram_tensor("kT_hbm", [4 * D, BLK], BF16, kind="Internal")
    acc_hbm = nc.dram_tensor("acc_hbm", [TPC, ACCW], F32, kind="Internal")
    qb_in = nc.dram_tensor("qb_in", [4 * D, BLK], BF16, kind="Internal")
    vb_in = nc.dram_tensor("vb_in", [2 * TPC, D], BF16, kind="Internal")
    qb_out = nc.dram_tensor("qb_out", [2 * NCORES * 2 * D, BLK], BF16, kind="Internal")
    vb_out = nc.dram_tensor("vb_out", [2 * NCORES * TPC, D], BF16, kind="Internal")
    wfcq_hbm = nc.dram_tensor("wfcq_hbm", [128, DCH * HCH * 128], FP8, kind="Internal")
    wprjq_hbm = nc.dram_tensor("wprjq_hbm", [128, HCH * D], BF16, kind="Internal")

    octx = ExitStack()
    E = octx.enter_context

    sync, gp_raw, te, ve_raw, ac_raw = nc.sync, nc.gpsimd, nc.tensor, nc.vector, nc.scalar
    ve = EngWrap(ve_raw)
    ac = EngWrap(ac_raw)
    gp = EngWrap(gp_raw)
    ts_, tt_, stt_ = ve.tensor_scalar, ve.tensor_tensor, ve.scalar_tensor_tensor

    LD = Sem(nc, octx, "LD")
    GD = Sem(nc, octx, "GD")
    PE = Sem(nc, octx, "PE")
    VE = FenceSem(nc, octx, "VE", ve_raw, ve)
    AC = FenceSem(nc, octx, "AC", ac_raw, ac)
    GP = Sem(nc, octx, "GP")
    CC = Sem(nc, octx, "CC")
    KQ = Sem(nc, octx, "KQ")
    VB = Sem(nc, octx, "VB")
    SC = Sem(nc, octx, "SC")
    ALLSEMS = (LD, GD, PE, VE, AC, GP, CC, KQ, VB, SC)

    def barrier():
        for eng in (sync, gp, te, ve, ac):
            for s in ALLSEMS:
                if s.n:
                    eng.wait_ge(s.h, s.n)

    def trans(out_ap, in_ap, first, last):
        p = in_ap.shape[0]
        return te.matmul(out_ap, in_ap, ident[0:p, 0:p],
                         start=first, stop=last, is_transpose=True)

    ones_col = E(nc.sbuf_tensor("ones_col", [128, 1], F32))
    ones_row = E(nc.sbuf_tensor("ones_row", [1, 128], F32))
    half_row = E(nc.sbuf_tensor("half_row", [1, 128], F32))
    ident = E(nc.sbuf_tensor("ident", [128, 128], F32))
    miota = E(nc.sbuf_tensor("miota", [128, 2 * BLK + 128], I16))
    s_dneg = E(nc.sbuf_tensor("s_dneg", [128, 17], F32))
    s_qidx = E(nc.sbuf_tensor("s_qidx", [128, 17 * DCH * 2], I32))
    s_vidx = E(nc.sbuf_tensor("s_vidx", [128, 17 * NIC * 2], I32))
    s_kidx = E(nc.sbuf_tensor("s_kidx", [128, 17 * DCH * 2], I32))
    s_aidx = E(nc.sbuf_tensor("s_aidx", [128, 17 * NIC], I32))
    s_mv = E(nc.sbuf_tensor("s_mv", [128, NCH], I32))
    rot_c = E(nc.sbuf_tensor("rot_c", [128, NCH], F32))
    rot_s = E(nc.sbuf_tensor("rot_s", [128, NCH], F32))
    stepc = [E(nc.sbuf_tensor(f"stepc{j}", [128, NCH], F32)) for j in range(9)]
    steps = [E(nc.sbuf_tensor(f"steps{j}", [128, NCH], F32)) for j in range(9)]
    lnw_sb = E(nc.sbuf_tensor("lnw_sb", [128, HCH], F32))
    lnb_sb = E(nc.sbuf_tensor("lnb_sb", [128, HCH], F32))
    bk_b = E(nc.sbuf_tensor("bk_b", [128, D], F32))
    bq_b = E(nc.sbuf_tensor("bq_b", [128, D], F32))
    bv_b = E(nc.sbuf_tensor("bv_b", [128, D], F32))
    bp_b = E(nc.sbuf_tensor("bp_b", [128, D], F32))
    wW_b = E(nc.sbuf_tensor("wW_b", [128, D], F32))
    bW_b = E(nc.sbuf_tensor("bW_b", [128, D], F32))
    s_tau = E(nc.sbuf_tensor("s_tau", [128, 4], F32))
    svec = E(nc.sbuf_tensor("svec", [128, 64], F32))
    s_mu = E(nc.sbuf_tensor("s_mu", [128, NCH], F32))
    s_rs = E(nc.sbuf_tensor("s_rs", [128, NCH], F32))
    s_hmu = E(nc.sbuf_tensor("s_hmu", [128, NCH], F32))
    s_hrs = E(nc.sbuf_tensor("s_hrs", [128, NCH], F32))
    vt1 = E(nc.sbuf_tensor("vt1", [128, NCH], F32))
    vt2 = E(nc.sbuf_tensor("vt2", [128, NCH], F32))
    vt3 = E(nc.sbuf_tensor("vt3", [128, NCH], F32))
    bt_i = E(nc.sbuf_tensor("bt_i", [128, NCH], I32))
    bt_f = E(nc.sbuf_tensor("bt_f", [128, NCH], F32))

    pb = [E(nc.psum_tensor(f"pb{i}", [128, 512], F32)) for i in range(8)]
    E(nc.Block())

    def emit_rsqrt(out_ap, u_ap, w1, w2, iters=16):
        ts_(w1, u_ap, 1.0, 0.5, Alu.add, Alu.mult)
        for _ in range(iters):
            ve.reciprocal(w2, w1)
            tt_(w2, w2, u_ap, Alu.mult)
            tt_(w1, w1, w2, Alu.add)
            ts_(w1, w1, 0.5, None, Alu.mult)
        ve.reciprocal(out_ap, w1)

    def ffn_pass(wfcq_sb, wprjq_sb, write_y, tag):
        GRPl, TGl = GRP, TG
        with nc.sbuf_tensor(f"xg{tag}", [128, GRPl * D], F32) as xg, \
             nc.sbuf_tensor(f"xsc{tag}", [128, GRPl * D], F32) as xsc, \
             nc.sbuf_tensor(f"xlnTh{tag}", [128, DCH * TGl], BF16) as xlnT_h, \
             nc.sbuf_tensor(f"xlnTl{tag}", [128, DCH * TGl], BF16) as xlnT_l, \
             nc.sbuf_tensor(f"gbuf{tag}", [128, 2 * TGl], F32) as gbuf, \
             nc.sbuf_tensor(f"gwh{tag}", [128, HCH * TGl], BF16) as gw_h, \
             nc.sbuf_tensor(f"gwl{tag}", [128, HCH * TGl], BF16) as gw_l, \
             nc.sbuf_tensor(f"g2s{tag}", [128, TGl], F32) as g2s, \
             nc.sbuf_tensor(f"strow{tag}", [1, 2 * TGl], F32) as strow:
            bank_free = {0: 0, 1: 0}
            slot_ve = [0, 0]
            bank5_free = 0
            last_hstat = 0
            for g in range(NG):
                ch0 = g * GRPl
                sync.wait_ge(VE.h, VE.n)
                sync.wait_ge(LD.h, LD.n)
                n0 = LD.inc(sync.dma_start(
                    out=xg[:, :].rearrange("p (a c) -> p a c", a=GRPl),
                    in_=x_hbm[ch0 * 128:(ch0 + GRPl) * 128, :].rearrange(
                        "(a p) c -> p a c", p=128)), 16)
                ve.wait_ge(LD.h, n0)
                vn = None
                for i in range(GRPl):
                    ch = ch0 + i
                    ts_(xsc[:, i * D:(i + 1) * D], xg[:, i * D:(i + 1) * D], 0.0, None,
                        Alu.add, Alu.add, accum_out=s_mu[:, ch:ch + 1])
                    vn = VE.inc(stt_(xsc[:, i * D:(i + 1) * D], xg[:, i * D:(i + 1) * D], 0.0,
                                     xg[:, i * D:(i + 1) * D], Alu.add, Alu.mult,
                                     accum_out=s_rs[:, ch:ch + 1]))
                sync.wait_ge(VE.h, vn)
            ts_(s_mu[:, 0:NCH], s_mu[:, 0:NCH], 1.0 / D, None, Alu.mult)
            ts_(s_rs[:, 0:NCH], s_rs[:, 0:NCH], 1.0 / D, None, Alu.mult)
            tt_(vt1[:, 0:NCH], s_mu[:, 0:NCH], s_mu[:, 0:NCH], Alu.mult)
            tt_(s_rs[:, 0:NCH], s_rs[:, 0:NCH], vt1[:, 0:NCH], Alu.subtract)
            ts_(s_rs[:, 0:NCH], s_rs[:, 0:NCH], LN_EPS, None, Alu.add)
            emit_rsqrt(s_rs[:, 0:NCH], s_rs[:, 0:NCH], vt1[:, 0:NCH], vt2[:, 0:NCH])
            for g in range(NG):
                ch0 = g * GRPl
                sync.wait_ge(VE.h, VE.n)
                sync.wait_ge(LD.h, LD.n)
                n0 = LD.inc(sync.dma_start(
                    out=xg[:, :].rearrange("p (a c) -> p a c", a=GRPl), in_=x_hbm[ch0 * 128:(ch0 + GRPl) * 128, :].rearrange(
                        "(a p) c -> p a c", p=128)), 16)
                ve.wait_ge(LD.h, n0)
                for i in range(GRPl):
                    ch = ch0 + i
                    vn = VE.inc(ts_(xsc[:, i * D:(i + 1) * D], xg[:, i * D:(i + 1) * D],
                                    s_mu[:, ch:ch + 1], s_rs[:, ch:ch + 1],
                                    Alu.subtract, Alu.mult))
                    te.wait_ge(VE.h, vn)
                    for dq in range(0, DCH, 4):
                        for t4 in range(4):
                            dc = dq + t4
                            PE.inc(trans(pb[4][:, t4 * 128:(t4 + 1) * 128],
                                         xsc[:, i * D + dc * 128:i * D + (dc + 1) * 128],
                                         first=(t4 == 0), last=(t4 == 3)))
                        ve.wait_ge(PE.h, PE.n)
                        vn = None
                        for t4 in range(4):
                            dc = dq + t4
                            col = dc * TGl + i * 128
                            ve.tensor_copy(xlnT_h[:, col:col + 128],
                                           pb[4][:, t4 * 128:(t4 + 1) * 128])
                            vn = VE.inc(tt_(xlnT_l[:, col:col + 128],
                                            pb[4][:, t4 * 128:(t4 + 1) * 128],
                                            xlnT_h[:, col:col + 128], Alu.subtract))
                        te.wait_ge(VE.h, vn)
                te.wait_ge(VE.h, bank5_free)
                for hc in range(HCH):
                    sl = hc % 2
                    bank = pb[sl]
                    te.wait_ge(AC.h, bank_free[sl])
                    for p, srcx in ((0, xlnT_h), (1, xlnT_l)):
                        for dc in range(DCH):
                            PE.inc(te.matmul(
                                bank[:, 0:TGl],
                                wfcq_sb[:, (dc * HCH + hc) * 128:(dc * HCH + hc + 1) * 128],
                                srcx[:, dc * TGl:(dc + 1) * TGl],
                                start=(p == 0 and dc == 0),
                                stop=(p == 1 and dc == DCH - 1)))
                    ac.wait_ge(PE.h, PE.n)
                    ac.wait_ge(VE.h, slot_ve[sl])
                    AC.inc(ac.activation(gbuf[:, sl * TGl:(sl + 1) * TGl], bank[:, 0:TGl],
                                         AF.Gelu))
                    ac.wait_ge(PE.h, last_hstat)
                    an = AC.inc(ac.activation(g2s[:, 0:TGl], gbuf[:, sl * TGl:(sl + 1) * TGl],
                                              AF.Square))
                    bank_free[sl] = an
                    te.wait_ge(AC.h, an)
                    PE.inc(te.matmul(pb[5][0:1, 0:TGl], ones_col[:, :],
                                     gbuf[:, sl * TGl:(sl + 1) * TGl],
                                     start=(hc == 0), stop=(hc == HCH - 1),
                                     skip_group_check=True))
                    last_hstat = PE.inc(te.matmul(pb[5][0:1, TGl:2 * TGl], ones_col[:, :],
                                                  g2s[:, 0:TGl],
                                                  start=False, stop=(hc == HCH - 1),
                                                  skip_group_check=True))
                    ve.wait_ge(AC.h, an - 1)
                    ts_(gw_h[:, hc * TGl:(hc + 1) * TGl], gbuf[:, sl * TGl:(sl + 1) * TGl],
                        lnw_sb[:, hc:hc + 1], None, Alu.mult)
                    slot_ve[sl] = VE.inc(stt_(gw_l[:, hc * TGl:(hc + 1) * TGl],
                                              gbuf[:, sl * TGl:(sl + 1) * TGl],
                                              lnw_sb[:, hc:hc + 1],
                                              gw_h[:, hc * TGl:(hc + 1) * TGl],
                                              Alu.mult, Alu.subtract))
                ve.wait_ge(PE.h, last_hstat)
                vn = VE.inc(ve.tensor_copy(strow[0:1, 0:2 * TGl], pb[5][0:1, 0:2 * TGl]))
                bank5_free = vn
                te.wait_ge(VE.h, vn)
                for i in range(GRPl):
                    ch = ch0 + i
                    PE.inc(trans(pb[6][:, 0:1], strow[0:1, i * 128:(i + 1) * 128],
                                 first=True, last=False))
                    pn = PE.inc(trans(pb[6][:, 1:2], strow[0:1, TGl + i * 128:TGl + (i + 1) * 128],
                                      first=False, last=True))
                    ve.wait_ge(PE.h, pn)
                    ts_(s_hmu[:, ch:ch + 1], pb[6][:, 0:1], 1.0 / H, None, Alu.mult)
                    vn = VE.inc(ts_(s_hrs[:, ch:ch + 1], pb[6][:, 1:2], 1.0 / H, None, Alu.mult))
                    te.wait_ge(VE.h, vn)
                for i in range(GRPl):
                    ch = ch0 + i
                    tt_(vt1[:, 0:1], s_hmu[:, ch:ch + 1], s_hmu[:, ch:ch + 1], Alu.mult)
                    tt_(s_hrs[:, ch:ch + 1], s_hrs[:, ch:ch + 1], vt1[:, 0:1], Alu.subtract)
                    ts_(s_hrs[:, ch:ch + 1], s_hrs[:, ch:ch + 1], LN_EPS, None, Alu.add)
                emit_rsqrt(s_hrs[:, ch0:ch0 + GRPl], s_hrs[:, ch0:ch0 + GRPl],
                           vt1[:, 0:GRPl], vt2[:, 0:GRPl])
                ts_(vt1[:, 0:GRPl], s_hrs[:, ch0:ch0 + GRPl], 0.5, None, Alu.mult)
                tt_(vt2[:, 0:GRPl], s_hmu[:, ch0:ch0 + GRPl], s_hrs[:, ch0:ch0 + GRPl], Alu.mult)
                VE.inc(ts_(vt2[:, 0:GRPl], vt2[:, 0:GRPl], -1.0, None, Alu.mult))
                for i in range(GRPl):
                    for dh in range(2):
                        bank = pb[2 + dh]
                        for p, srcx in ((0, gw_h), (1, gw_l)):
                            for hc in range(HCH):
                                PE.inc(te.matmul(
                                    bank[:, 0:512],
                                    srcx[:, hc * TGl + i * 128:hc * TGl + (i + 1) * 128],
                                    wprjq_sb[:, hc * D + dh * 512:hc * D + (dh + 1) * 512],
                                    start=(p == 0 and hc == 0),
                                    stop=(p == 1 and hc == HCH - 1)))
                        ve.wait_ge(PE.h, PE.n)
                        xs = xg[:, i * D + dh * 512:i * D + (dh + 1) * 512]
                        stt_(xs, bank[:, 0:512], vt1[:, i:i + 1], xs, Alu.mult, Alu.add)
                        stt_(xs, wW_b[:, dh * 512:(dh + 1) * 512], vt2[:, i:i + 1], xs,
                             Alu.mult, Alu.add)
                        vn = VE.inc(tt_(xs, xs, bW_b[:, dh * 512:(dh + 1) * 512], Alu.add))
                        te.wait_ge(VE.h, vn)
                sync.wait_ge(VE.h, VE.n)
                LD.inc(sync.dma_start(
                    out=x_hbm[ch0 * 128:(ch0 + GRPl) * 128, :].rearrange(
                        "(a p) c -> p a c", p=128), in_=xg[:, :].rearrange("p (a c) -> p a c", a=GRPl)), 16)
                if write_y:
                    LD.inc(sync.dma_start(
                        out=y_out[ch0 * 128:(ch0 + GRPl) * 128, :].rearrange(
                            "(a p) c -> p a c", p=128), in_=xg[:, :].rearrange("p (a c) -> p a c", a=GRPl)), 16)
        barrier()


    ve.memset(ones_col[:, :], 1.0)
    ve.memset(ones_row[:, :], 1.0)
    ve.memset(half_row[:, :], 0.5)
    with nc.sbuf_tensor("iota_p", [128, 128], I32) as iota_p, \
         nc.sbuf_tensor("iota_f", [128, 128], I32) as iota_f:
        gp.iota(iota_p[:, :], [[0, 128]], base=0, channel_multiplier=1).then_inc(GP.h, 1); GP.n += 1
        gp.iota(iota_f[:, :], [[1, 128]], base=0, channel_multiplier=0).then_inc(GP.h, 1); GP.n += 1
        gp.iota(miota[:, :], [[1, 2 * BLK + 128]], base=-BLK, channel_multiplier=-1,
                allow_small_or_imprecise_dtypes=True).then_inc(GP.h, 1); GP.n += 1
        ve.wait_ge(GP.h, 2)
        VE.inc(tt_(ident[:, :], iota_p[:, :], iota_f[:, :], Alu.is_equal))

        for dst, src in ((s_dneg, dneg_d), (s_qidx, qidx_d), (s_vidx, vidx_d),
                         (s_kidx, kidx_d), (s_aidx, aidx_d), (s_mv, mvals_d),
                         (lnw_sb, lnw_d), (lnb_sb, lnb_d)):
            LD.inc(sync.dma_start(out=dst[:, :], in_=src), 16)
        ld_tables = LD.n

        with nc.sbuf_tensor("s_brow", [1, D], F32) as s_brow:
            for bd, bb in ((bk_d, bk_b), (bq_d, bq_b), (bv_d, bv_b), (bp_d, bp_b)):
                n0 = LD.inc(sync.dma_start(out=s_brow[:, :], in_=bd), 16)
                te.wait_ge(LD.h, n0)
                te.wait_ge(VE.h, VE.n)
                for hh in range(2):
                    pn = PE.inc(te.matmul(pb[6][:, 0:512], ones_row[:, :],
                                          s_brow[0:1, hh * 512:(hh + 1) * 512],
                                          start=True, stop=True))
                    ve.wait_ge(PE.h, pn)
                    vn = VE.inc(ve.tensor_copy(bb[:, hh * 512:(hh + 1) * 512], pb[6][:, 0:512]))
                    te.wait_ge(VE.h, vn)
                sync.wait_ge(VE.h, VE.n)

        ve.wait_ge(LD.h, LD.n)
        ve.memset(rot_c[:, :], 1.0)
        ve.memset(rot_s[:, :], 0.0)
        for j in range(14):
            cj = float(np.cos(np.float64(2 ** j))); sj = float(np.sin(np.float64(2 ** j)))
            ts_(bt_i[:, :], s_mv[:, :], j, None, Alu.arith_shift_right)
            ts_(bt_i[:, :], bt_i[:, :], 1, None, Alu.bitwise_and)
            ts_(bt_f[:, :], bt_i[:, :], -1.0, None, Alu.mult)
            ts_(vt1[:, :], rot_c[:, :], cj, None, Alu.mult)
            stt_(vt1[:, :], rot_s[:, :], -sj, vt1[:, :], Alu.mult, Alu.add)
            ts_(vt2[:, :], rot_c[:, :], sj, None, Alu.mult)
            stt_(vt2[:, :], rot_s[:, :], cj, vt2[:, :], Alu.mult, Alu.add)
            tt_(vt3[:, :], vt1[:, :], rot_c[:, :], Alu.subtract)
            tt_(vt3[:, :], vt3[:, :], bt_f[:, :], Alu.mult)
            tt_(rot_c[:, :], rot_c[:, :], vt3[:, :], Alu.subtract)
            tt_(vt3[:, :], vt2[:, :], rot_s[:, :], Alu.subtract)
            tt_(vt3[:, :], vt3[:, :], bt_f[:, :], Alu.mult)
            tt_(rot_s[:, :], rot_s[:, :], vt3[:, :], Alu.subtract)
        ve.tensor_copy(stepc[0][:, :], rot_c[:, :])
        ve.tensor_copy(steps[0][:, :], rot_s[:, :])
        for j in range(1, 9):
            tt_(vt1[:, :], stepc[j - 1][:, :], stepc[j - 1][:, :], Alu.mult)
            tt_(vt2[:, :], steps[j - 1][:, :], steps[j - 1][:, :], Alu.mult)
            tt_(stepc[j][:, :], vt1[:, :], vt2[:, :], Alu.subtract)
            tt_(vt1[:, :], stepc[j - 1][:, :], steps[j - 1][:, :], Alu.mult)
            ts_(steps[j][:, :], vt1[:, :], 2.0, None, Alu.mult)
        VE.inc()

    try:
        with nc.sbuf_tensor("wfcq", [128, DCH * HCH * 128], FP8) as wfcq, \
             nc.sbuf_tensor("wprjq", [128, HCH * D], BF16) as wprjq:
            with nc.sbuf_tensor("wpiece", [128, H], F32) as wpiece, \
                 nc.sbuf_tensor("wabs", [128, 1024], F32) as wabs, \
                 nc.sbuf_tensor("asum", [128, 64], F32) as asum, \
                 nc.sbuf_tensor("qtile", [128, 512], F32) as qtile, \
                 nc.sbuf_tensor("qpos", [128, 512], F32) as qpos, \
                 nc.sbuf_tensor("lnwb4", [128, 4 * HCH], BF16) as lnwb4, \
                 nc.sbuf_tensor("strow2", [1, 1024], F32) as strow2:

              def gamma_pass(w_dram, nrow, tau_col):
                  ncol = (H * D) // nrow
                  ac.wait_ge(VE.h, VE.n)
                  sync.wait_ge(VE.h, VE.n)
                  acc_cols = 0
                  for rc in range(nrow // 128):
                      n0 = LD.inc(sync.dma_start(out=wpiece[:, 0:ncol],
                                                 in_=w_dram[rc * 128:(rc + 1) * 128, :]), 16)
                      ac.wait_ge(LD.h, n0)
                      an = None
                      for gcol in range(ncol // 1024):
                          an = AC.inc(ac.activation(wabs[:, 0:1024],
                                                    wpiece[:, gcol * 1024:(gcol + 1) * 1024], AF.Abs,
                                                    accum_out=asum[:, acc_cols:acc_cols + 1]))
                          acc_cols += 1
                      sync.wait_ge(AC.h, an)
                  ve.wait_ge(AC.h, AC.n)
                  vn = VE.inc(ve.tensor_reduce(svec[:, 0:1], asum[:, 0:acc_cols],
                                               axis=mybir.AxisListType.X, op=Alu.add))
                  te.wait_ge(VE.h, vn)
                  pn = PE.inc(te.matmul(pb[6][0:1, 0:1], ones_col[:, :], svec[:, 0:1],
                                        start=True, stop=True))
                  ve.wait_ge(PE.h, pn)
                  vn = VE.inc(ve.tensor_copy(svec[0:1, 2:3], pb[6][0:1, 0:1]))
                  te.wait_ge(VE.h, vn)
                  pn = PE.inc(te.matmul(pb[6][:, 0:1], ones_row[:, :], svec[0:1, 2:3],
                                        start=True, stop=True))
                  ve.wait_ge(PE.h, pn)
                  ts_(s_tau[:, tau_col:tau_col + 1], pb[6][:, 0:1], 1.0 / (H * D), QEPS,
                      Alu.mult, Alu.add)
                  ts_(s_tau[:, tau_col:tau_col + 1], s_tau[:, tau_col:tau_col + 1], 0.5, None, Alu.mult)
                  VE.inc(ts_(s_tau[:, 2 + tau_col:3 + tau_col], s_tau[:, tau_col:tau_col + 1],
                             -1.0, None, Alu.mult))
                  te.wait_ge(VE.h, VE.n)

              gamma_pass(Wfc_d, H, 0)
              gamma_pass(Wproj_d, D, 1)

              def quant_transpose(w_dram, nrow, tau_col, dst_sb, dst_col_of):
                  ncol = (H * D) // nrow
                  for rc in range(nrow // 128):
                      sync.wait_ge(AC.h, AC.n)
                      sync.wait_ge(VE.h, VE.n)
                      n0 = LD.inc(sync.dma_start(out=wpiece[:, 0:ncol],
                                                 in_=w_dram[rc * 128:(rc + 1) * 128, :]), 16)
                      te.wait_ge(LD.h, n0)
                      for cc4 in range(0, ncol // 128, 4):
                          te.wait_ge(VE.h, VE.n)
                          for i in range(4):
                              cc = cc4 + i
                              PE.inc(trans(pb[4][:, i * 128:(i + 1) * 128],
                                           wpiece[:, cc * 128:(cc + 1) * 128],
                                           first=(i == 0), last=(i == 3)))
                          ve.wait_ge(PE.h, PE.n)
                          ts_(qpos[:, 0:512], pb[4][:, 0:512], s_tau[:, tau_col:tau_col + 1],
                              None, Alu.is_ge)
                          ts_(qtile[:, 0:512], pb[4][:, 0:512], s_tau[:, 2 + tau_col:3 + tau_col],
                              None, Alu.is_le)
                          for i in range(4):
                              cc = cc4 + i
                              dcol = dst_col_of(rc, cc)
                              VE.inc(tt_(dst_sb[:, dcol:dcol + 128], qpos[:, i * 128:(i + 1) * 128],
                                         qtile[:, i * 128:(i + 1) * 128], Alu.subtract))

              quant_transpose(Wfc_d, H, 0, wfcq, lambda rc, cc: (cc * HCH + rc) * 128)
              quant_transpose(Wproj_d, D, 1, wprjq, lambda rc, cc: cc * D + rc * 128)

              ve.wait_ge(LD.h, LD.n)
              for hc in range(HCH):
                  c0 = 4 * hc
                  ts_(lnwb4[:, c0:c0 + 1], lnw_sb[:, hc:hc + 1], 0.0, None, Alu.add)
                  stt_(lnwb4[:, c0 + 1:c0 + 2], lnw_sb[:, hc:hc + 1], -1.0,
                       lnwb4[:, c0:c0 + 1], Alu.mult, Alu.add)
                  ts_(lnwb4[:, c0 + 1:c0 + 2], lnwb4[:, c0 + 1:c0 + 2], -1.0, None, Alu.mult)
                  ts_(lnwb4[:, c0 + 2:c0 + 3], lnb_sb[:, hc:hc + 1], 0.0, None, Alu.add)
                  stt_(lnwb4[:, c0 + 3:c0 + 4], lnb_sb[:, hc:hc + 1], -1.0,
                       lnwb4[:, c0 + 2:c0 + 3], Alu.mult, Alu.add)
                  ts_(lnwb4[:, c0 + 3:c0 + 4], lnwb4[:, c0 + 3:c0 + 4], -1.0, None, Alu.mult)
              vn = VE.inc()
              te.wait_ge(VE.h, vn)
              for dh in range(2):
                  for which, dst in ((0, wW_b), (1, bW_b)):
                      for p2 in range(2):
                          col = which * 2 + p2
                          for hc in range(HCH):
                              PE.inc(te.matmul(pb[6][0:1, 0:512],
                                               lnwb4[:, 4 * hc + col:4 * hc + col + 1],
                                               wprjq[:, hc * D + dh * 512:hc * D + (dh + 1) * 512],
                                               start=(p2 == 0 and hc == 0),
                                               stop=(p2 == 1 and hc == HCH - 1)))
                      ve.wait_ge(PE.h, PE.n)
                      vn = VE.inc(ve.tensor_copy(strow2[0:1, 0:512], pb[6][0:1, 0:512]))
                      te.wait_ge(VE.h, vn)
                      pn = PE.inc(te.matmul(pb[6][:, 0:512], half_row[:, :],
                                            strow2[0:1, 0:512], start=True, stop=True))
                      ve.wait_ge(PE.h, pn)
                      vn = VE.inc(ve.tensor_copy(dst[:, dh * 512:(dh + 1) * 512], pb[6][:, 0:512]))
                      te.wait_ge(VE.h, vn)

              sync.wait_ge(VE.h, VE.n)
              LD.inc(sync.dma_start(out=wfcq_hbm[:, :], in_=wfcq[:, :]), 16)
              LD.inc(sync.dma_start(out=wprjq_hbm[:, :], in_=wprjq[:, :]), 16)
              LD.inc(sync.dma_start(out=x_hbm[:, :], in_=x_in), 16)
              with nc.sbuf_tensor("zer", [128, ACCW], F32) as zer:
                  vn = VE.inc(ve.memset(zer[:, :], 0.0))
                  sync.wait_ge(VE.h, vn)
                  for i in range(TPC // 128):
                      LD.inc(sync.dma_start(out=acc_hbm[i * 128:(i + 1) * 128, :], in_=zer[:, :]), 16)
              barrier()

            if stop_after != "prep":
                ffn_pass(wfcq, wprjq, write_y=False, tag="a")
            if stop_after in ("prep", "ffn1"):
                with nc.sbuf_tensor("dmp", [128, GRP * D], F32) as dmp:
                    for g in range(NG):
                        n0 = LD.inc(sync.dma_start(
                            out=dmp[:, :].rearrange("p (a c) -> p a c", a=GRP),
                            in_=x_hbm[g * GRP * 128:(g + 1) * GRP * 128, :].rearrange(
                                "(a p) c -> p a c", p=128)), 16)
                        sync.wait_ge(LD.h, n0)
                        n1 = LD.inc(sync.dma_start(
                            out=y_out[g * GRP * 128:(g + 1) * GRP * 128, :].rearrange(
                                "(a p) c -> p a c", p=128),
                            in_=dmp[:, :].rearrange("p (a c) -> p a c", a=GRP)), 16)
                        sync.wait_ge(LD.h, n1)
                raise _Stop


        with ExitStack() as p3ctx:
            E3 = p3ctx.enter_context
            w3h = E3(nc.sbuf_tensor("w3h", [128, 3 * DCH * D], BF16))
            w3l = E3(nc.sbuf_tensor("w3l", [128, 3 * DCH * D], BF16))
            xg2 = E3(nc.sbuf_tensor("xg2", [128, GRP * D], F32))
            xsc2 = E3(nc.sbuf_tensor("xsc2", [128, GRP * D], F32))
            ln2Th = E3(nc.sbuf_tensor("ln2Th", [128, DCH * 128], BF16))
            ln2Tl = E3(nc.sbuf_tensor("ln2Tl", [128, DCH * 128], BF16))
            kqv = E3(nc.sbuf_tensor("kqv", [128, 3 * D], F32))
            krot = E3(nc.sbuf_tensor("krot", [128, 2 * D], F32))
            tabc = E3(nc.sbuf_tensor("tabc", [128, 512], F32))
            tabs = E3(nc.sbuf_tensor("tabs", [128, 512], F32))
            rtmp = E3(nc.sbuf_tensor("rtmp", [128, 512], F32))
            rtmp2 = E3(nc.sbuf_tensor("rtmp2", [128, 512], F32))
            tph = E3(nc.sbuf_tensor("tph", [128, 8 * 128], BF16))
            tpl = E3(nc.sbuf_tensor("tpl", [128, 8 * 128], BF16))
            HD2 = DCH // 2
            with nc.sbuf_tensor("wtmp", [128, HD2 * D], F32) as wtmp:
                for wi, wd in enumerate((Wk_d, Wq_d, Wv_d)):
                    for hf in range(2):
                        sync.wait_ge(VE.h, VE.n)
                        n0 = LD.inc(sync.dma_start(
                            out=wtmp[:, :].rearrange("p (a c) -> p a c", a=HD2),
                            in_=wd[hf * HD2 * 128:(hf + 1) * HD2 * 128, :].rearrange(
                                "(a p) c -> p a c", p=128)), 16)
                        ve.wait_ge(LD.h, n0)
                        o = wi * DCH * D + hf * HD2 * D
                        ve.tensor_copy(w3h[:, o:o + HD2 * D], wtmp[:, :])
                        VE.inc(tt_(w3l[:, o:o + HD2 * D], wtmp[:, :], w3h[:, o:o + HD2 * D],
                                   Alu.subtract))
                VE.inc()
                ve.wait_ge(VE.h, VE.n)
            for g in range(NG):
                ch0 = g * GRP
                sync.wait_ge(VE.h, VE.n)
                sync.wait_ge(LD.h, LD.n)
                n0 = LD.inc(sync.dma_start(
                    out=xg2[:, :].rearrange("p (a c) -> p a c", a=GRP),
                    in_=x_hbm[ch0 * 128:(ch0 + GRP) * 128, :].rearrange(
                        "(a p) c -> p a c", p=128)), 16)
                ve.wait_ge(LD.h, n0)
                vn = None
                for i in range(GRP):
                    ch = ch0 + i
                    ts_(xsc2[:, i * D:(i + 1) * D], xg2[:, i * D:(i + 1) * D], 0.0, None,
                        Alu.add, Alu.add, accum_out=s_mu[:, ch:ch + 1])
                    vn = VE.inc(stt_(xsc2[:, i * D:(i + 1) * D], xg2[:, i * D:(i + 1) * D], 0.0,
                                     xg2[:, i * D:(i + 1) * D], Alu.add, Alu.mult,
                                     accum_out=s_rs[:, ch:ch + 1]))
                sync.wait_ge(VE.h, vn)
            ts_(s_mu[:, 0:NCH], s_mu[:, 0:NCH], 1.0 / D, None, Alu.mult)
            ts_(s_rs[:, 0:NCH], s_rs[:, 0:NCH], 1.0 / D, None, Alu.mult)
            tt_(vt1[:, 0:NCH], s_mu[:, 0:NCH], s_mu[:, 0:NCH], Alu.mult)
            tt_(s_rs[:, 0:NCH], s_rs[:, 0:NCH], vt1[:, 0:NCH], Alu.subtract)
            ts_(s_rs[:, 0:NCH], s_rs[:, 0:NCH], LN_EPS, None, Alu.add)
            emit_rsqrt(s_rs[:, 0:NCH], s_rs[:, 0:NCH], vt1[:, 0:NCH], vt2[:, 0:NCH])
            for g in range(NG):
                if g == NG // 2:
                    gp.wait_ge(LD.h, LD.n)
                    gp.collective_compute("AllGather", Alu.bypass,
                                          replica_groups=[list(range(NCORES))],
                                          ins=[qb_in[0:2 * D, :].opt()],
                                          outs=[qb_out[0:NCORES * 2 * D, :].opt()]
                                          ).then_inc(CC.h, 1)
                    CC.n += 1
                    gp.collective_compute("AllGather", Alu.bypass,
                                          replica_groups=[list(range(NCORES))],
                                          ins=[vb_in[0:TPC, :].opt()],
                                          outs=[vb_out[0:NCORES * TPC, :].opt()]
                                          ).then_inc(CC.h, 1)
                    CC.n += 1
                ch0 = g * GRP
                sync.wait_ge(VE.h, VE.n)
                sync.wait_ge(LD.h, LD.n)
                n0 = LD.inc(sync.dma_start(
                    out=xg2[:, :].rearrange("p (a c) -> p a c", a=GRP),
                    in_=x_hbm[ch0 * 128:(ch0 + GRP) * 128, :].rearrange(
                        "(a p) c -> p a c", p=128)), 16)
                ve.wait_ge(LD.h, n0)
                for i in range(GRP):
                    ch = ch0 + i
                    blk_i = ch // NIC
                    tok0 = (ch % NIC) * 128
                    vn = VE.inc(ts_(xsc2[:, i * D:(i + 1) * D], xg2[:, i * D:(i + 1) * D],
                                    s_mu[:, ch:ch + 1], s_rs[:, ch:ch + 1],
                                    Alu.subtract, Alu.mult))
                    te.wait_ge(VE.h, vn)
                    for dq in range(0, DCH, 4):
                        for t4 in range(4):
                            dc = dq + t4
                            PE.inc(trans(pb[4][:, t4 * 128:(t4 + 1) * 128],
                                         xsc2[:, i * D + dc * 128:i * D + (dc + 1) * 128],
                                         first=(t4 == 0), last=(t4 == 3)))
                        ve.wait_ge(PE.h, PE.n)
                        vn = None
                        for t4 in range(4):
                            dc = dq + t4
                            ve.tensor_copy(ln2Th[:, dc * 128:(dc + 1) * 128],
                                           pb[4][:, t4 * 128:(t4 + 1) * 128])
                            vn = VE.inc(tt_(ln2Tl[:, dc * 128:(dc + 1) * 128],
                                            pb[4][:, t4 * 128:(t4 + 1) * 128],
                                            ln2Th[:, dc * 128:(dc + 1) * 128], Alu.subtract))
                        te.wait_ge(VE.h, vn)
                    ve.wait_ge(LD.h, LD.n)
                    for wi in range(3):
                        for dh in range(2):
                            bank = pb[(wi * 2 + dh) % 4]
                            wof = wi * DCH * D + dh * 512
                            np_ = 0
                            passes = ((ln2Th, w3h), (ln2Th, w3l), (ln2Tl, w3h))
                            for pi, (lt, wt) in enumerate(passes):
                                for dc in range(DCH):
                                    np_ += 1
                                    PE.inc(te.matmul(bank[:, 0:512],
                                                     lt[:, dc * 128:(dc + 1) * 128],
                                                     wt[:, wof + dc * D:wof + dc * D + 512],
                                                     start=(pi == 0 and dc == 0),
                                                     stop=(pi == 2 and dc == DCH - 1)))
                            pn = PE.n
                            ve.wait_ge(PE.h, pn)
                            dst = kqv[:, wi * D + dh * 512:wi * D + (dh + 1) * 512]
                            if wi == 0:
                                vn = VE.inc(tt_(dst, bank[:, 0:512],
                                                bk_b[:, dh * 512:(dh + 1) * 512], Alu.add))
                            elif wi == 1:
                                vn = VE.inc(tt_(dst, bank[:, 0:512],
                                                bq_b[:, dh * 512:(dh + 1) * 512], Alu.add))
                            else:
                                vn = VE.inc(ve.tensor_copy(dst, bank[:, 0:512]))
                            te.wait_ge(VE.h, vn)
                    ve.memset(tabc[:, 0:1], 1.0)
                    ve.memset(tabs[:, 0:1], 0.0)
                    for r in range(9):
                        n = 1 << r
                        sc = stepc[r][:, ch:ch + 1]; ss = steps[r][:, ch:ch + 1]
                        ts_(rtmp[:, 0:n], tabc[:, 0:n], sc, None, Alu.mult)
                        ts_(rtmp2[:, 0:n], tabs[:, 0:n], ss, None, Alu.mult)
                        tt_(tabc[:, n:2 * n], rtmp[:, 0:n], rtmp2[:, 0:n], Alu.subtract)
                        ts_(rtmp[:, 0:n], tabc[:, 0:n], ss, None, Alu.mult)
                        stt_(tabs[:, n:2 * n], tabs[:, 0:n], sc, rtmp[:, 0:n], Alu.mult, Alu.add)
                    for t_i in range(2):
                        srcv = kqv[:, t_i * D:(t_i + 1) * D]
                        dstv = krot[:, t_i * D:(t_i + 1) * D]
                        z0 = srcv[:, 0:D:2]; z1 = srcv[:, 1:D:2]
                        r0 = dstv[:, 0:D:2]; r1 = dstv[:, 1:D:2]
                        tt_(rtmp[:, 0:512], tabc[:, :], z0, Alu.mult)
                        tt_(rtmp2[:, 0:512], tabs[:, :], z1, Alu.mult)
                        tt_(r0, rtmp[:, 0:512], rtmp2[:, 0:512], Alu.add)
                        tt_(rtmp[:, 0:512], tabs[:, :], z0, Alu.mult)
                        tt_(rtmp2[:, 0:512], tabc[:, :], z1, Alu.mult)
                        VE.inc(tt_(r1, rtmp2[:, 0:512], rtmp[:, 0:512], Alu.subtract))
                    for t_i, dst_hbm in ((0, kT_hbm), (1, qb_in)):
                        te.wait_ge(VE.h, VE.n)
                        ve.wait_ge(LD.h, LD.n)
                        for dq in range(0, DCH, 4):
                            for t4 in range(4):
                                dc = dq + t4
                                PE.inc(trans(pb[4][:, t4 * 128:(t4 + 1) * 128],
                                             krot[:, t_i * D + dc * 128:t_i * D + (dc + 1) * 128],
                                             first=(t4 == 0), last=(t4 == 3)))
                            ve.wait_ge(PE.h, PE.n)
                            vn = None
                            for t4 in range(4):
                                dc = dq + t4
                                ve.tensor_copy(tph[:, dc * 128:(dc + 1) * 128],
                                               pb[4][:, t4 * 128:(t4 + 1) * 128])
                                vn = VE.inc(tt_(tpl[:, dc * 128:(dc + 1) * 128],
                                                pb[4][:, t4 * 128:(t4 + 1) * 128],
                                                tph[:, dc * 128:(dc + 1) * 128], Alu.subtract))
                            te.wait_ge(VE.h, vn)
                        sync.wait_ge(VE.h, VE.n)
                        for sec, tpx in ((0, tph), (1, tpl)):
                            base = (blk_i * 2 * D + sec * D) if t_i == 1 else (sec * 2 * D + blk_i * D)
                            for dc in range(DCH):
                                LD.inc(sync.dma_start(
                                    out=dst_hbm[base + dc * 128:base + (dc + 1) * 128,
                                                tok0:tok0 + 128],
                                    in_=tpx[:, dc * 128:(dc + 1) * 128]), 16)
                    vsrc = kqv[:, 2 * D:3 * D]
                    ve.wait_ge(LD.h, LD.n)
                    ve.tensor_copy(tph[:, 0:D], vsrc)
                    VE.inc(tt_(tpl[:, 0:D], vsrc, tph[:, 0:D], Alu.subtract))
                    sync.wait_ge(VE.h, VE.n)
                    vrow0 = blk_i * TPC + tok0
                    LD.inc(sync.dma_start(out=vb_in[vrow0:vrow0 + 128, :],
                                          in_=tph[:, 0:D]), 16)
                    LD.inc(sync.dma_start(out=vb_in[vrow0 + BLK:vrow0 + BLK + 128, :],
                                          in_=tpl[:, 0:D]), 16)
            barrier()

        if stop_after == "qkv":
            raise _Stop
        gp.wait_ge(LD.h, LD.n)
        gp.collective_compute("AllGather", Alu.bypass,
                              replica_groups=[list(range(NCORES))],
                              ins=[qb_in[2 * D:4 * D, :].opt()],
                              outs=[qb_out[NCORES * 2 * D:2 * NCORES * 2 * D, :].opt()]
                              ).then_inc(CC.h, 1)
        CC.n += 1
        gp.collective_compute("AllGather", Alu.bypass,
                              replica_groups=[list(range(NCORES))],
                              ins=[vb_in[TPC:2 * TPC, :].opt()],
                              outs=[vb_out[NCORES * TPC:2 * NCORES * TPC, :].opt()]
                              ).then_inc(CC.h, 1)
        CC.n += 1
        for eng in (sync, gp, te, ve, ac):
            for s in (LD, GD, PE, VE, AC, GP, KQ, VB, SC):
                if s.n:
                    eng.wait_ge(s.h, s.n)

        with ExitStack() as p5ctx:
            E5 = p5ctx.enter_context
            kTbh = E5(nc.sbuf_tensor("kTbh", [128, DCH * BLK], BF16))
            kTbl = E5(nc.sbuf_tensor("kTbl", [128, DCH * BLK], BF16))
            qTbh = E5(nc.sbuf_tensor("qTbh", [128, DCH * BLK], BF16))
            qTbl = E5(nc.sbuf_tensor("qTbl", [128, DCH * BLK], BF16))
            vbh = E5(nc.sbuf_tensor("vbh", [128, NIC * D], BF16))
            vbl = E5(nc.sbuf_tensor("vbl", [128, NIC * D], BF16))
            weih = E5(nc.sbuf_tensor("weih", [128, NJC * BLK], BF16))
            weil = E5(nc.sbuf_tensor("weil", [128, NJC * BLK], BF16))
            weif = E5(nc.sbuf_tensor("weif", [128, ICOL], F32))
            ones_bf = E5(nc.sbuf_tensor("ones_bf", [128, 1], BF16))
            mtile = E5(nc.sbuf_tensor("mtile", [128, ICOL], F32))
            stg0 = E5(nc.sbuf_tensor("stg0", [128, NIC * ACCW], F32))
            stgs = [stg0]
            ve.memset(ones_bf[:, :], 1.0)
            for sb in stgs:
                for ic in range(NIC):
                    ve.memset(sb[:, ic * ACCW + 1025:(ic + 1) * ACCW], 0.0)
            VE.inc()
            gp.wait_ge(VE.h, VE.n)
            kq_loaded = [0] * 17
            v_loaded = [0] * 17
            scores_done = [0] * 17
            wv_done = [0] * 17
            stg_free = [0]
            sp_done = [0] * 17

            def gather_kq(p):
                for sec, (kdst, qdst) in enumerate(((kTbh, qTbh), (kTbl, qTbl))):
                    for cc in range(DCH):
                        col = (p * DCH + cc) * 2 + sec
                        KQ.inc(gp.indirect_dma_start(
                            kdst[:, cc * BLK:(cc + 1) * BLK], None, kT_hbm[:, :],
                            IndirectOffsetOnAxis(ap=s_kidx[:, col:col + 1], axis=0)), 16)
                        KQ.inc(gp.indirect_dma_start(
                            qdst[:, cc * BLK:(cc + 1) * BLK], None, qb_out[:, :],
                            IndirectOffsetOnAxis(ap=s_qidx[:, col:col + 1], axis=0)), 16)
                return KQ.n

            def gather_v(p):
                for sec, vdst in enumerate((vbh, vbl)):
                    for tc in range(NIC):
                        col = (p * NIC + tc) * 2 + sec
                        VB.inc(gp.indirect_dma_start(
                            vdst[:, tc * D:(tc + 1) * D], None, vb_out[:, :],
                            IndirectOffsetOnAxis(ap=s_vidx[:, col:col + 1], axis=0)), 16)
                return VB.n

            gp.wait_ge(CC.h, 2)
            kq_loaded[0] = gather_kq(0)
            v_loaded[0] = gather_v(0)

            bankf = {0: 0, 1: 0, 2: 0, 3: 0}
            bank5_f = 0
            wv_pe_prev = 0
            for p in range(17):
                te.wait_ge(KQ.h, kq_loaded[p])
                ve.wait_ge(PE.h, wv_pe_prev)
                spn = None
                for jj in range(NJC):
                    for ih in range(NIT):
                        b = (jj * NIT + ih) % 2
                        bank = pb[b]
                        te.wait_ge(VE.h, bankf[b])
                        passes = ((qTbh, kTbh), (qTbh, kTbl), (qTbl, kTbh))
                        for pi, (qs, ks) in enumerate(passes):
                            for cc in range(DCH):
                                PE.inc(te.matmul(
                                    bank[:, 0:ICOL],
                                    qs[:, cc * BLK + jj * 128:cc * BLK + (jj + 1) * 128],
                                    ks[:, cc * BLK + ih * ICOL:cc * BLK + (ih + 1) * ICOL],
                                    start=(pi == 0 and cc == 0),
                                    stop=(pi == 2 and cc == DCH - 1)))
                        pn = PE.n
                        off = jj * 128 - ih * ICOL
                        y0 = BLK - off
                        ve.wait_ge(PE.h, pn)
                        ts_(mtile[:, 0:ICOL], miota[:, y0:y0 + ICOL], s_dneg[:, p:p + 1],
                            NEG_BIG, Alu.is_lt, Alu.mult)
                        vn = VE.inc(tt_(weif[:, 0:ICOL], bank[:, 0:ICOL],
                                        mtile[:, 0:ICOL], Alu.add))
                        bankf[b] = vn
                        ac.wait_ge(VE.h, vn)
                        ac.activation(weif[:, 0:ICOL], weif[:, 0:ICOL],
                                      AF.Exp, scale=INV_SQRT_C)
                        an = AC.inc(ac.activation(weif[:, 0:ICOL], weif[:, 0:ICOL],
                                                  AF.Ln, bias=1.0, scale=1.0))
                        wcol = jj * BLK + ih * ICOL
                        ve.wait_ge(AC.h, an)
                        ve.tensor_copy(weih[:, wcol:wcol + ICOL], weif[:, 0:ICOL])
                        spn = VE.inc(tt_(weil[:, wcol:wcol + ICOL], weif[:, 0:ICOL],
                                         weih[:, wcol:wcol + ICOL], Alu.subtract))
                sp_done[p] = spn
                scores_done[p] = PE.n
                if p + 1 < 17:
                    gp.wait_ge(PE.h, scores_done[p])
                    if p + 1 == 9:
                        gp.wait_ge(CC.h, 4)
                    kq_loaded[p + 1] = gather_kq(p + 1)
                stg = stgs[0]
                te.wait_ge(VE.h, sp_done[p])
                te.wait_ge(VB.h, v_loaded[p])
                ve.wait_ge(SC.h, stg_free[0])
                te.wait_ge(VE.h, bank5_f)
                first_R = True
                for ic in range(NIC):
                    for dh in range(2):
                        b = 2 + (ic * 2 + dh) % 2
                        bank = pb[b]
                        te.wait_ge(VE.h, bankf[b])
                        passes = ((weih, vbh), (weih, vbl), (weil, vbh))
                        for pi, (ws, vs) in enumerate(passes):
                            for jj in range(NJC):
                                PE.inc(te.matmul(
                                    bank[:, 0:512],
                                    ws[:, jj * BLK + ic * 128:jj * BLK + (ic + 1) * 128],
                                    vs[:, jj * D + dh * 512:jj * D + (dh + 1) * 512],
                                    start=(pi == 0 and jj == 0),
                                    stop=(pi == 2 and jj == NJC - 1)))
                                if dh == 0 and pi <= 1:
                                    wsr = (weih, weil)[pi]
                                    PE.inc(te.matmul(
                                        pb[5][:, ic:ic + 1],
                                        wsr[:, jj * BLK + ic * 128:jj * BLK + (ic + 1) * 128],
                                        ones_bf[:, :],
                                        start=first_R, stop=False,
                                        skip_group_check=True))
                                    first_R = False
                        pn = PE.n
                        ve.wait_ge(PE.h, pn)
                        vn = VE.inc(ve.tensor_copy(
                            stg[:, ic * ACCW + dh * 512:ic * ACCW + (dh + 1) * 512],
                            bank[:, 0:512]))
                        bankf[b] = vn
                ve.wait_ge(PE.h, PE.n)
                vn = VE.inc(ve.tensor_copy(stg[:, 1024:(NIC - 1) * ACCW + 1025:ACCW],
                                           pb[5][:, 0:NIC]))
                bank5_f = vn
                wv_done[p] = VE.n
                if p + 1 < 17:
                    gp.wait_ge(VE.h, wv_done[p])
                    v_loaded[p + 1] = gather_v(p + 1)
                gp.wait_ge(VE.h, wv_done[p])
                for ic in range(NIC):
                    SC.inc(gp.indirect_dma_start(
                        acc_hbm[:, :],
                        IndirectOffsetOnAxis(ap=s_aidx[:, p * NIC + ic:p * NIC + ic + 1], axis=0),
                        stg[:, ic * ACCW:(ic + 1) * ACCW], None,
                        compute_op=Alu.add), 16)
                    gp.wait_ge(SC.h, SC.n)
                stg_free[0] = SC.n
                wv_pe_prev = PE.n
            barrier()

        if stop_after == "attn":
            raise _Stop
        with nc.sbuf_tensor("wp", [128, DCH * D], F32) as wp, \
             nc.sbuf_tensor("accs", [128, ACCW], F32) as accs, \
             nc.sbuf_tensor("att", [128, D], F32) as att, \
             nc.sbuf_tensor("attT", [128, DCH * 128], F32) as attT, \
             nc.sbuf_tensor("xg3", [128, D], F32) as xg3:
            LD.inc(sync.dma_start(out=wp[:, :].rearrange("p (a c) -> p a c", a=DCH),
                                  in_=Wp_d.rearrange("(a p) c -> p a c", p=128)), 16)
            ld_wp = LD.n
            te.wait_ge(LD.h, ld_wp)
            for ch in range(NCH):
                sync.wait_ge(VE.h, VE.n)
                sync.wait_ge(LD.h, LD.n)
                n0 = LD.inc(sync.dma_start(out=accs[:, :],
                                           in_=acc_hbm[ch * 128:(ch + 1) * 128, :]), 16)
                n1 = LD.inc(sync.dma_start(out=xg3[:, :],
                                           in_=x_hbm[ch * 128:(ch + 1) * 128, :]), 16)
                ve.wait_ge(LD.h, n1)
                stt_(att[:, 0:512], bv_b[:, 0:512], accs[:, 1024:1025], accs[:, 0:512],
                     Alu.mult, Alu.add)
                vn = VE.inc(stt_(att[:, 512:1024], bv_b[:, 512:1024], accs[:, 1024:1025],
                                 accs[:, 512:1024], Alu.mult, Alu.add))
                te.wait_ge(VE.h, vn)
                for dq in range(0, DCH, 4):
                    for t4 in range(4):
                        dc = dq + t4
                        PE.inc(trans(pb[4][:, t4 * 128:(t4 + 1) * 128],
                                     att[:, dc * 128:(dc + 1) * 128],
                                     first=(t4 == 0), last=(t4 == 3)))
                    ve.wait_ge(PE.h, PE.n)
                    vn = None
                    for t4 in range(4):
                        vn = VE.inc(ve.tensor_copy(attT[:, (dq + t4) * 128:(dq + t4 + 1) * 128],
                                                   pb[4][:, t4 * 128:(t4 + 1) * 128]))
                    te.wait_ge(VE.h, vn)
                for dh in range(2):
                    bank = pb[dh]
                    for dc in range(DCH):
                        PE.inc(te.matmul(bank[:, 0:512], attT[:, dc * 128:(dc + 1) * 128],
                                         wp[:, dc * D + dh * 512:dc * D + (dh + 1) * 512],
                                         start=(dc == 0), stop=(dc == DCH - 1)))
                    pn = PE.n
                    ve.wait_ge(PE.h, pn)
                    xs = xg3[:, dh * 512:(dh + 1) * 512]
                    tt_(xs, xs, bank[:, 0:512], Alu.add)
                    vn = VE.inc(tt_(xs, xs, bp_b[:, dh * 512:(dh + 1) * 512], Alu.add))
                    te.wait_ge(VE.h, vn)
                sync.wait_ge(VE.h, VE.n)
                LD.inc(sync.dma_start(out=x_hbm[ch * 128:(ch + 1) * 128, :], in_=xg3[:, :]), 16)
            barrier()

        if stop_after == "pproj":
            raise _Stop
        with nc.sbuf_tensor("wfcq2", [128, DCH * HCH * 128], FP8) as wfcq2, \
             nc.sbuf_tensor("wprjq2", [128, HCH * D], BF16) as wprjq2:
            LD.inc(sync.dma_start(out=wfcq2[:, :], in_=wfcq_hbm[:, :]), 16)
            LD.inc(sync.dma_start(out=wprjq2[:, :], in_=wprjq_hbm[:, :]), 16)
            wld = LD.n
            te.wait_ge(LD.h, wld)
            ve.wait_ge(LD.h, wld)

            ffn_pass(wfcq2, wprjq2, write_y=True, tag="b")

    except _Stop:
        pass

    sync.wait_ge(LD.h, LD.n)
    sync.wait_ge(GD.h, GD.n)
    sync.wait_ge(SC.h, SC.n)
    try:
        octx.close()
    except AssertionError:
        pass
    return nc



def _pairs_for_core(c):
    bA, bB = c, NBLK - 1 - c
    return ([(J, 0, bA) for J in range(bA + 1)]
            + [(J, 1, bB) for J in range(bB + 1)])


def _core_tables(c, cfg):
    BLK, NIC = cfg.BLK, cfg.NIC
    pairs = _pairs_for_core(c)
    assert len(pairs) == 17
    pp = np.arange(128)
    qidx = np.zeros((128, 17 * DCH * 2), np.int32)
    kidx = np.zeros((128, 17 * DCH * 2), np.int32)
    vidx = np.zeros((128, 17 * NIC * 2), np.int32)
    aidx = np.zeros((128, 17 * NIC), np.int32)
    dneg = np.zeros((128, 17), np.float32)
    for p, (J, s_I, I_blk) in enumerate(pairs):
        oJ = J if J < NCORES else NBLK - 1 - J
        sJ = 0 if J < NCORES else 1
        for cc in range(DCH):
            for sec in range(2):
                qidx[:, (p * DCH + cc) * 2 + sec] = (
                    (sJ * NCORES + oJ) * 2 * D + sec * D + cc * 128 + pp)
                kidx[:, (p * DCH + cc) * 2 + sec] = (
                    sec * 2 * D + s_I * D + cc * 128 + pp)
        for tc in range(NIC):
            for sec in range(2):
                vidx[:, (p * NIC + tc) * 2 + sec] = (
                    (sJ * NCORES + oJ) * cfg.TPC + sec * BLK + tc * 128 + pp)
            aidx[:, p * NIC + tc] = s_I * BLK + tc * 128 + pp
        dneg[:, p] = -float((I_blk - J) * BLK)
    return qidx, kidx, vidx, aidx, dneg


def _shard_host(cfg, inputs):
    T = cfg.T
    x = np.ascontiguousarray(np.asarray(inputs["L0_attn_x"], np.float32).reshape(T, D))
    lnw = np.ascontiguousarray(np.asarray(inputs["L0_ffn_ln_w"], np.float32).reshape(HCH, 128).T)
    lnb = np.ascontiguousarray(np.asarray(inputs["L0_ffn_ln_b"], np.float32).reshape(HCH, 128).T)
    common = {
        "Wfc": np.ascontiguousarray(np.asarray(inputs["L0_ffn_W_fc"], np.float32)),
        "Wproj": np.ascontiguousarray(np.asarray(inputs["L0_ffn_W_proj"], np.float32)),
        "Wk": np.ascontiguousarray(np.asarray(inputs["L0_attn_Wk"], np.float32)),
        "Wq": np.ascontiguousarray(np.asarray(inputs["L0_attn_Wq"], np.float32)),
        "Wv": np.ascontiguousarray(np.asarray(inputs["L0_attn_Wv"], np.float32)),
        "Wp": np.ascontiguousarray(np.asarray(inputs["L0_attn_Wp"], np.float32)),
        "bk": np.ascontiguousarray(np.asarray(inputs["L0_attn_bk"], np.float32).reshape(1, D)),
        "bq": np.ascontiguousarray(np.asarray(inputs["L0_attn_bq"], np.float32).reshape(1, D)),
        "bv": np.ascontiguousarray(np.asarray(inputs["L0_attn_bv"], np.float32).reshape(1, D)),
        "bp": np.ascontiguousarray(np.asarray(inputs["L0_attn_bp"], np.float32).reshape(1, D)),
        "lnw_t": lnw, "lnb_t": lnb,
    }
    in_maps = []
    for c in range(NCORES):
        bA, bB = c, NBLK - 1 - c
        xs = np.concatenate([x[bA * cfg.BLK:(bA + 1) * cfg.BLK],
                             x[bB * cfg.BLK:(bB + 1) * cfg.BLK]], axis=0)
        qidx, kidx, vidx, aidx, dneg = _core_tables(c, cfg)
        toks = np.concatenate([np.arange(bA * cfg.BLK, (bA + 1) * cfg.BLK),
                               np.arange(bB * cfg.BLK, (bB + 1) * cfg.BLK)])
        m = (cfg.T - 1 - toks).astype(np.int32).reshape(cfg.NCH, 128).T
        im = dict(common)
        im["x_in"] = np.ascontiguousarray(xs)
        im["m_vals"] = np.ascontiguousarray(m)
        im["dneg"] = dneg
        im["qidx"] = qidx; im["kidx"] = kidx; im["vidx"] = vidx; im["aidx"] = aidx
        in_maps.append(im)
    return in_maps


def _gather_host(cfg, results):
    T = cfg.T
    y = np.zeros((T, D), np.float32)
    for c in range(NCORES):
        bA, bB = c, NBLK - 1 - c
        yc = results[c]["y"]
        y[bA * cfg.BLK:(bA + 1) * cfg.BLK] = yc[:cfg.BLK]
        y[bB * cfg.BLK:(bB + 1) * cfg.BLK] = yc[cfg.BLK:]
    return y.reshape(1, T, D)


def kernel(**inputs):
    T = int(np.asarray(inputs["L0_attn_x"]).shape[1])
    cfg = Cfg(T)
    nc = build_program(cfg)
    in_maps = _shard_host(cfg, inputs)
    res = run_bass_kernel_spmd(nc, in_maps, core_ids=list(range(NCORES)))
    return _gather_host(cfg, res.results)


# version 05 (final)
# speedup vs baseline: 1.1600x; 1.0033x over previous
import numpy as np
import ml_dtypes
import concourse.bass as bass
import concourse.mybir as mybir
from concourse.bass_utils import run_bass_kernel_spmd
from bass_rust import IndirectOffsetOnAxis
from contextlib import ExitStack

F32 = mybir.dt.float32
BF16 = mybir.dt.bfloat16
FP8 = mybir.dt.float8e4
I32 = mybir.dt.int32
I16 = mybir.dt.int16
AF = mybir.ActivationFunctionType
Alu = mybir.AluOpType

NCORES = 8
D = 1024
H = 4096
DCH = D // 128
HCH = H // 128
NBLK = 16
LN_EPS = 1e-5
QEPS = 1e-6
ACCW = 1088
INV_SQRT_C = 1.0 / 32.0
NEG_BIG = -1e30


class _Stop(Exception):
    pass


class Cfg:
    def __init__(self, T):
        self.T = T
        self.BLK = T // NBLK
        self.TPC = 2 * self.BLK
        self.NCH = self.TPC // 128
        self.NJC = self.BLK // 128
        self.ICOL = min(512, self.BLK)
        self.NIT = self.BLK // self.ICOL
        self.NIC = self.BLK // 128
        self.GRP = 2
        self.NG = self.NCH // self.GRP
        self.TG = 128 * self.GRP


class Sem:
    def __init__(self, nc, ctx, name):
        self.h = ctx.enter_context(nc.semaphore(name))
        self.n = 0

    def inc(self, instr, kk=1):
        instr.then_inc(self.h, kk)
        self.n += kk
        return self.n


class FenceSem(Sem):
    def __init__(self, nc, ctx, name, eng, wrap):
        super().__init__(nc, ctx, name)
        self._eng = eng
        self._wrap = wrap

    def inc(self, instr=None, kk=1):
        self._eng.drain()
        if self._wrap is not None:
            self._wrap._written.clear()
            self._wrap._read.clear()
        self._eng.nop().then_inc(self.h, kk)
        self.n += kk
        return self.n




class EngWrap:
    _OPS = ("tensor_scalar", "tensor_tensor", "scalar_tensor_tensor", "tensor_copy",
            "memset", "reciprocal", "tensor_reduce", "activation", "iota",
            "tensor_add", "tensor_sub", "tensor_mul", "memzero")

    def __init__(self, eng):
        self._e = eng
        self._written = set()
        self._read = set()

    def __getattr__(self, name):
        attr = getattr(self._e, name)
        if name not in self._OPS:
            return attr

        def call(*args, **kwargs):
            names_in = []
            names_out = []
            for i, a in enumerate(args):
                if hasattr(a, "tensor") and hasattr(a.tensor, "name"):
                    (names_out if i == 0 else names_in).append(a.tensor.name)
            for kk, a in kwargs.items():
                if hasattr(a, "tensor") and hasattr(a.tensor, "name"):
                    (names_out if kk == "accum_out" else names_in).append(a.tensor.name)
            if (any(n in self._written for n in names_in + names_out)
                    or any(n in self._read for n in names_out)):
                self._e.drain()
                self._written.clear()
                self._read.clear()
            r = attr(*args, **kwargs)
            self._written.update(names_out)
            self._read.update(names_in)
            return r

        return call


def build_program(cfg, stop_after="full"):
    nc = bass.Bass("TRN2", target_bir_lowering=False)
    BLK, TPC, NCH, GRP, TG, NG = cfg.BLK, cfg.TPC, cfg.NCH, cfg.GRP, cfg.TG, cfg.NG
    NJC, ICOL, NIT, NIC = cfg.NJC, cfg.ICOL, cfg.NIT, cfg.NIC

    def din(name, shape, dt=F32):
        return nc.dram_tensor(name, shape, dt, kind="ExternalInput").ap()

    x_in = din("x_in", [TPC, D])
    Wfc_d = din("Wfc", [H, D]); Wproj_d = din("Wproj", [D, H])
    Wk_d = din("Wk", [D, D]); Wq_d = din("Wq", [D, D]); Wv_d = din("Wv", [D, D]); Wp_d = din("Wp", [D, D])
    bk_d = din("bk", [1, D]); bq_d = din("bq", [1, D]); bv_d = din("bv", [1, D]); bp_d = din("bp", [1, D])
    lnw_d = din("lnw_t", [128, HCH]); lnb_d = din("lnb_t", [128, HCH])
    mvals_d = din("m_vals", [128, NCH], I32)
    dneg_d = din("dneg", [128, 17])
    qidx_d = din("qidx", [128, 17 * DCH * 2], I32)
    vidx_d = din("vidx", [128, 17 * NIC * 2], I32)
    kidx_d = din("kidx", [128, 17 * DCH * 2], I32)
    aidx_d = din("aidx", [128, 17 * NIC], I32)
    y_out = nc.dram_tensor("y", [TPC, D], F32, kind="ExternalOutput").ap()

    x_hbm = nc.dram_tensor("x_hbm", [TPC, D], F32, kind="Internal")
    kT_hbm = nc.dram_tensor("kT_hbm", [4 * D, BLK], BF16, kind="Internal")
    acc_hbm = nc.dram_tensor("acc_hbm", [TPC, ACCW], F32, kind="Internal")
    qb_in = nc.dram_tensor("qb_in", [4 * D, BLK], BF16, kind="Internal")
    vb_in = nc.dram_tensor("vb_in", [2 * TPC, D], BF16, kind="Internal")
    qb_out = nc.dram_tensor("qb_out", [2 * NCORES * 2 * D, BLK], BF16, kind="Internal")
    vb_out = nc.dram_tensor("vb_out", [2 * NCORES * TPC, D], BF16, kind="Internal")
    wfcq_hbm = nc.dram_tensor("wfcq_hbm", [128, DCH * HCH * 128], FP8, kind="Internal")
    wprjq_hbm = nc.dram_tensor("wprjq_hbm", [128, HCH * D], BF16, kind="Internal")

    octx = ExitStack()
    E = octx.enter_context

    sync, gp_raw, te, ve_raw, ac_raw = nc.sync, nc.gpsimd, nc.tensor, nc.vector, nc.scalar
    ve = EngWrap(ve_raw)
    ac = EngWrap(ac_raw)
    gp = EngWrap(gp_raw)
    ts_, tt_, stt_ = ve.tensor_scalar, ve.tensor_tensor, ve.scalar_tensor_tensor

    LD = Sem(nc, octx, "LD")
    GD = Sem(nc, octx, "GD")
    PE = Sem(nc, octx, "PE")
    VE = FenceSem(nc, octx, "VE", ve_raw, ve)
    AC = FenceSem(nc, octx, "AC", ac_raw, ac)
    GP = Sem(nc, octx, "GP")
    CC = Sem(nc, octx, "CC")
    KQ = Sem(nc, octx, "KQ")
    VB = Sem(nc, octx, "VB")
    SC = Sem(nc, octx, "SC")
    ALLSEMS = (LD, GD, PE, VE, AC, GP, CC, KQ, VB, SC)

    def barrier():
        for eng in (sync, gp, te, ve, ac):
            for s in ALLSEMS:
                if s.n:
                    eng.wait_ge(s.h, s.n)

    def trans(out_ap, in_ap, first, last):
        p = in_ap.shape[0]
        return te.matmul(out_ap, in_ap, ident[0:p, 0:p],
                         start=first, stop=last, is_transpose=True)

    ones_col = E(nc.sbuf_tensor("ones_col", [128, 1], F32))
    ones_row = E(nc.sbuf_tensor("ones_row", [1, 128], F32))
    half_row = E(nc.sbuf_tensor("half_row", [1, 128], F32))
    ident = E(nc.sbuf_tensor("ident", [128, 128], F32))
    miota = E(nc.sbuf_tensor("miota", [128, 2 * BLK + 128], I16))
    s_dneg = E(nc.sbuf_tensor("s_dneg", [128, 17], F32))
    s_qidx = E(nc.sbuf_tensor("s_qidx", [128, 17 * DCH * 2], I32))
    s_vidx = E(nc.sbuf_tensor("s_vidx", [128, 17 * NIC * 2], I32))
    s_kidx = E(nc.sbuf_tensor("s_kidx", [128, 17 * DCH * 2], I32))
    s_aidx = E(nc.sbuf_tensor("s_aidx", [128, 17 * NIC], I32))
    s_mv = E(nc.sbuf_tensor("s_mv", [128, NCH], I32))
    rot_c = E(nc.sbuf_tensor("rot_c", [128, NCH], F32))
    rot_s = E(nc.sbuf_tensor("rot_s", [128, NCH], F32))
    stepc = [E(nc.sbuf_tensor(f"stepc{j}", [128, NCH], F32)) for j in range(9)]
    steps = [E(nc.sbuf_tensor(f"steps{j}", [128, NCH], F32)) for j in range(9)]
    lnw_sb = E(nc.sbuf_tensor("lnw_sb", [128, HCH], F32))
    lnb_sb = E(nc.sbuf_tensor("lnb_sb", [128, HCH], F32))
    bk_b = E(nc.sbuf_tensor("bk_b", [128, D], F32))
    bq_b = E(nc.sbuf_tensor("bq_b", [128, D], F32))
    bv_b = E(nc.sbuf_tensor("bv_b", [128, D], F32))
    bp_b = E(nc.sbuf_tensor("bp_b", [128, D], F32))
    wW_b = E(nc.sbuf_tensor("wW_b", [128, D], F32))
    bW_b = E(nc.sbuf_tensor("bW_b", [128, D], F32))
    s_tau = E(nc.sbuf_tensor("s_tau", [128, 4], F32))
    svec = E(nc.sbuf_tensor("svec", [128, 64], F32))
    s_mu = E(nc.sbuf_tensor("s_mu", [128, NCH], F32))
    s_rs = E(nc.sbuf_tensor("s_rs", [128, NCH], F32))
    s_mu2 = E(nc.sbuf_tensor("s_mu2", [128, NCH], F32))
    s_rs2 = E(nc.sbuf_tensor("s_rs2", [128, NCH], F32))
    s_hmu = E(nc.sbuf_tensor("s_hmu", [128, NCH], F32))
    s_hrs = E(nc.sbuf_tensor("s_hrs", [128, NCH], F32))
    vt1 = E(nc.sbuf_tensor("vt1", [128, NCH], F32))
    vt2 = E(nc.sbuf_tensor("vt2", [128, NCH], F32))
    vt3 = E(nc.sbuf_tensor("vt3", [128, NCH], F32))
    bt_i = E(nc.sbuf_tensor("bt_i", [128, NCH], I32))
    bt_f = E(nc.sbuf_tensor("bt_f", [128, NCH], F32))

    pb = [E(nc.psum_tensor(f"pb{i}", [128, 512], F32)) for i in range(8)]
    E(nc.Block())

    def emit_rsqrt(out_ap, u_ap, w1, w2, iters=16):
        ts_(w1, u_ap, 1.0, 0.5, Alu.add, Alu.mult)
        for _ in range(iters):
            ve.reciprocal(w2, w1)
            tt_(w2, w2, u_ap, Alu.mult)
            tt_(w1, w1, w2, Alu.add)
            ts_(w1, w1, 0.5, None, Alu.mult)
        ve.reciprocal(out_ap, w1)

    def ln_finish(mu_ap, rs_ap):
        ts_(mu_ap, mu_ap, 1.0 / D, None, Alu.mult)
        ts_(rs_ap, rs_ap, 1.0 / D, None, Alu.mult)
        tt_(vt1[:, 0:NCH], mu_ap, mu_ap, Alu.mult)
        tt_(rs_ap, rs_ap, vt1[:, 0:NCH], Alu.subtract)
        ts_(rs_ap, rs_ap, LN_EPS, None, Alu.add)
        emit_rsqrt(rs_ap, rs_ap, vt1[:, 0:NCH], vt2[:, 0:NCH])

    def ffn_pass(wfcq_sb, wprjq_sb, write_y, tag, stats_mu, stats_rs, dst_mu=None, dst_rs=None):
        GRPl, TGl = GRP, TG
        with nc.sbuf_tensor(f"xg{tag}", [128, GRPl * D], F32) as xg, \
             nc.sbuf_tensor(f"xsc{tag}", [128, GRPl * D], F32) as xsc, \
             nc.sbuf_tensor(f"xlnTh{tag}", [128, DCH * TGl], BF16) as xlnT_h, \
             nc.sbuf_tensor(f"xlnTl{tag}", [128, DCH * TGl], BF16) as xlnT_l, \
             nc.sbuf_tensor(f"gbuf{tag}", [128, 2 * TGl], F32) as gbuf, \
             nc.sbuf_tensor(f"gwh{tag}", [128, HCH * TGl], BF16) as gw_h, \
             nc.sbuf_tensor(f"gwl{tag}", [128, HCH * TGl], BF16) as gw_l, \
             nc.sbuf_tensor(f"g2s{tag}", [128, TGl], F32) as g2s, \
             nc.sbuf_tensor(f"strow{tag}", [1, 2 * TGl], F32) as strow:
            bank_free = {0: 0, 1: 0}
            slot_ve = [0, 0]
            bank5_free = 0
            last_hstat = 0
            ln_finish(stats_mu[:, 0:NCH], stats_rs[:, 0:NCH])
            for g in range(NG):
                ch0 = g * GRPl
                sync.wait_ge(VE.h, VE.n)
                sync.wait_ge(LD.h, LD.n)
                n0 = LD.inc(sync.dma_start(
                    out=xg[:, :].rearrange("p (a c) -> p a c", a=GRPl), in_=x_hbm[ch0 * 128:(ch0 + GRPl) * 128, :].rearrange(
                        "(a p) c -> p a c", p=128)), 16)
                ve.wait_ge(LD.h, n0)
                for i in range(GRPl):
                    ch = ch0 + i
                    vn = VE.inc(ts_(xsc[:, i * D:(i + 1) * D], xg[:, i * D:(i + 1) * D],
                                    stats_mu[:, ch:ch + 1], stats_rs[:, ch:ch + 1],
                                    Alu.subtract, Alu.mult))
                    te.wait_ge(VE.h, vn)
                    for dq in range(0, DCH, 4):
                        for t4 in range(4):
                            dc = dq + t4
                            PE.inc(trans(pb[4][:, t4 * 128:(t4 + 1) * 128],
                                         xsc[:, i * D + dc * 128:i * D + (dc + 1) * 128],
                                         first=(t4 == 0), last=(t4 == 3)))
                        ve.wait_ge(PE.h, PE.n)
                        vn = None
                        for t4 in range(4):
                            dc = dq + t4
                            col = dc * TGl + i * 128
                            ve.tensor_copy(xlnT_h[:, col:col + 128],
                                           pb[4][:, t4 * 128:(t4 + 1) * 128])
                            vn = VE.inc(tt_(xlnT_l[:, col:col + 128],
                                            pb[4][:, t4 * 128:(t4 + 1) * 128],
                                            xlnT_h[:, col:col + 128], Alu.subtract))
                        te.wait_ge(VE.h, vn)
                te.wait_ge(VE.h, bank5_free)
                for hc in range(HCH):
                    sl = hc % 2
                    bank = pb[sl]
                    te.wait_ge(AC.h, bank_free[sl])
                    for p, srcx in ((0, xlnT_h), (1, xlnT_l)):
                        for dc in range(DCH):
                            PE.inc(te.matmul(
                                bank[:, 0:TGl],
                                wfcq_sb[:, (dc * HCH + hc) * 128:(dc * HCH + hc + 1) * 128],
                                srcx[:, dc * TGl:(dc + 1) * TGl],
                                start=(p == 0 and dc == 0),
                                stop=(p == 1 and dc == DCH - 1)))
                    ac.wait_ge(PE.h, PE.n)
                    ac.wait_ge(VE.h, slot_ve[sl])
                    AC.inc(ac.activation(gbuf[:, sl * TGl:(sl + 1) * TGl], bank[:, 0:TGl],
                                         AF.Gelu))
                    ac.wait_ge(PE.h, last_hstat)
                    an = AC.inc(ac.activation(g2s[:, 0:TGl], gbuf[:, sl * TGl:(sl + 1) * TGl],
                                              AF.Square))
                    bank_free[sl] = an
                    te.wait_ge(AC.h, an)
                    PE.inc(te.matmul(pb[5][0:1, 0:TGl], ones_col[:, :],
                                     gbuf[:, sl * TGl:(sl + 1) * TGl],
                                     start=(hc == 0), stop=(hc == HCH - 1),
                                     skip_group_check=True))
                    last_hstat = PE.inc(te.matmul(pb[5][0:1, TGl:2 * TGl], ones_col[:, :],
                                                  g2s[:, 0:TGl],
                                                  start=False, stop=(hc == HCH - 1),
                                                  skip_group_check=True))
                    ve.wait_ge(AC.h, an - 1)
                    ts_(gw_h[:, hc * TGl:(hc + 1) * TGl], gbuf[:, sl * TGl:(sl + 1) * TGl],
                        lnw_sb[:, hc:hc + 1], None, Alu.mult)
                    slot_ve[sl] = VE.inc(stt_(gw_l[:, hc * TGl:(hc + 1) * TGl],
                                              gbuf[:, sl * TGl:(sl + 1) * TGl],
                                              lnw_sb[:, hc:hc + 1],
                                              gw_h[:, hc * TGl:(hc + 1) * TGl],
                                              Alu.mult, Alu.subtract))
                ve.wait_ge(PE.h, last_hstat)
                vn = VE.inc(ve.tensor_copy(strow[0:1, 0:2 * TGl], pb[5][0:1, 0:2 * TGl]))
                bank5_free = vn
                te.wait_ge(VE.h, vn)
                for i in range(GRPl):
                    ch = ch0 + i
                    PE.inc(trans(pb[6][:, 0:1], strow[0:1, i * 128:(i + 1) * 128],
                                 first=True, last=False))
                    pn = PE.inc(trans(pb[6][:, 1:2], strow[0:1, TGl + i * 128:TGl + (i + 1) * 128],
                                      first=False, last=True))
                    ve.wait_ge(PE.h, pn)
                    ts_(s_hmu[:, ch:ch + 1], pb[6][:, 0:1], 1.0 / H, None, Alu.mult)
                    vn = VE.inc(ts_(s_hrs[:, ch:ch + 1], pb[6][:, 1:2], 1.0 / H, None, Alu.mult))
                    te.wait_ge(VE.h, vn)
                for i in range(GRPl):
                    ch = ch0 + i
                    tt_(vt1[:, 0:1], s_hmu[:, ch:ch + 1], s_hmu[:, ch:ch + 1], Alu.mult)
                    tt_(s_hrs[:, ch:ch + 1], s_hrs[:, ch:ch + 1], vt1[:, 0:1], Alu.subtract)
                    ts_(s_hrs[:, ch:ch + 1], s_hrs[:, ch:ch + 1], LN_EPS, None, Alu.add)
                emit_rsqrt(s_hrs[:, ch0:ch0 + GRPl], s_hrs[:, ch0:ch0 + GRPl],
                           vt1[:, 0:GRPl], vt2[:, 0:GRPl])
                ts_(vt1[:, 0:GRPl], s_hrs[:, ch0:ch0 + GRPl], 0.5, None, Alu.mult)
                tt_(vt2[:, 0:GRPl], s_hmu[:, ch0:ch0 + GRPl], s_hrs[:, ch0:ch0 + GRPl], Alu.mult)
                VE.inc(ts_(vt2[:, 0:GRPl], vt2[:, 0:GRPl], -1.0, None, Alu.mult))
                for i in range(GRPl):
                    for dh in range(2):
                        bank = pb[2 + dh]
                        for p, srcx in ((0, gw_h), (1, gw_l)):
                            for hc in range(HCH):
                                PE.inc(te.matmul(
                                    bank[:, 0:512],
                                    srcx[:, hc * TGl + i * 128:hc * TGl + (i + 1) * 128],
                                    wprjq_sb[:, hc * D + dh * 512:hc * D + (dh + 1) * 512],
                                    start=(p == 0 and hc == 0),
                                    stop=(p == 1 and hc == HCH - 1)))
                        ve.wait_ge(PE.h, PE.n)
                        xs = xg[:, i * D + dh * 512:i * D + (dh + 1) * 512]
                        stt_(xs, bank[:, 0:512], vt1[:, i:i + 1], xs, Alu.mult, Alu.add)
                        stt_(xs, wW_b[:, dh * 512:(dh + 1) * 512], vt2[:, i:i + 1], xs,
                             Alu.mult, Alu.add)
                        vn = VE.inc(tt_(xs, xs, bW_b[:, dh * 512:(dh + 1) * 512], Alu.add))
                        te.wait_ge(VE.h, vn)
                if dst_mu is not None:
                    for i in range(GRPl):
                        ch = ch0 + i
                        ts_(xsc[:, i * D:(i + 1) * D], xg[:, i * D:(i + 1) * D], 0.0, None,
                            Alu.add, Alu.add, accum_out=dst_mu[:, ch:ch + 1])
                        stt_(xsc[:, i * D:(i + 1) * D], xg[:, i * D:(i + 1) * D], 0.0,
                             xg[:, i * D:(i + 1) * D], Alu.add, Alu.mult,
                             accum_out=dst_rs[:, ch:ch + 1])
                    VE.inc()
                sync.wait_ge(VE.h, VE.n)
                LD.inc(sync.dma_start(
                    out=x_hbm[ch0 * 128:(ch0 + GRPl) * 128, :].rearrange(
                        "(a p) c -> p a c", p=128), in_=xg[:, :].rearrange("p (a c) -> p a c", a=GRPl)), 16)
                if write_y:
                    LD.inc(sync.dma_start(
                        out=y_out[ch0 * 128:(ch0 + GRPl) * 128, :].rearrange(
                            "(a p) c -> p a c", p=128), in_=xg[:, :].rearrange("p (a c) -> p a c", a=GRPl)), 16)
        barrier()


    ve.memset(ones_col[:, :], 1.0)
    ve.memset(ones_row[:, :], 1.0)
    ve.memset(half_row[:, :], 0.5)
    with nc.sbuf_tensor("iota_p", [128, 128], I32) as iota_p, \
         nc.sbuf_tensor("iota_f", [128, 128], I32) as iota_f:
        gp.iota(iota_p[:, :], [[0, 128]], base=0, channel_multiplier=1).then_inc(GP.h, 1); GP.n += 1
        gp.iota(iota_f[:, :], [[1, 128]], base=0, channel_multiplier=0).then_inc(GP.h, 1); GP.n += 1
        gp.iota(miota[:, :], [[1, 2 * BLK + 128]], base=-BLK, channel_multiplier=-1,
                allow_small_or_imprecise_dtypes=True).then_inc(GP.h, 1); GP.n += 1
        ve.wait_ge(GP.h, 2)
        VE.inc(tt_(ident[:, :], iota_p[:, :], iota_f[:, :], Alu.is_equal))

        for dst, src in ((s_dneg, dneg_d), (s_qidx, qidx_d), (s_vidx, vidx_d),
                         (s_kidx, kidx_d), (s_aidx, aidx_d), (s_mv, mvals_d),
                         (lnw_sb, lnw_d), (lnb_sb, lnb_d)):
            LD.inc(sync.dma_start(out=dst[:, :], in_=src), 16)
        ld_tables = LD.n

        with nc.sbuf_tensor("s_brow", [1, D], F32) as s_brow:
            for bd, bb in ((bk_d, bk_b), (bq_d, bq_b), (bv_d, bv_b), (bp_d, bp_b)):
                n0 = LD.inc(sync.dma_start(out=s_brow[:, :], in_=bd), 16)
                te.wait_ge(LD.h, n0)
                te.wait_ge(VE.h, VE.n)
                for hh in range(2):
                    pn = PE.inc(te.matmul(pb[6][:, 0:512], ones_row[:, :],
                                          s_brow[0:1, hh * 512:(hh + 1) * 512],
                                          start=True, stop=True))
                    ve.wait_ge(PE.h, pn)
                    vn = VE.inc(ve.tensor_copy(bb[:, hh * 512:(hh + 1) * 512], pb[6][:, 0:512]))
                    te.wait_ge(VE.h, vn)
                sync.wait_ge(VE.h, VE.n)

        ve.wait_ge(LD.h, LD.n)
        ve.memset(rot_c[:, :], 1.0)
        ve.memset(rot_s[:, :], 0.0)
        for j in range(14):
            cj = float(np.cos(np.float64(2 ** j))); sj = float(np.sin(np.float64(2 ** j)))
            ts_(bt_i[:, :], s_mv[:, :], j, None, Alu.arith_shift_right)
            ts_(bt_i[:, :], bt_i[:, :], 1, None, Alu.bitwise_and)
            ts_(bt_f[:, :], bt_i[:, :], -1.0, None, Alu.mult)
            ts_(vt1[:, :], rot_c[:, :], cj, None, Alu.mult)
            stt_(vt1[:, :], rot_s[:, :], -sj, vt1[:, :], Alu.mult, Alu.add)
            ts_(vt2[:, :], rot_c[:, :], sj, None, Alu.mult)
            stt_(vt2[:, :], rot_s[:, :], cj, vt2[:, :], Alu.mult, Alu.add)
            tt_(vt3[:, :], vt1[:, :], rot_c[:, :], Alu.subtract)
            tt_(vt3[:, :], vt3[:, :], bt_f[:, :], Alu.mult)
            tt_(rot_c[:, :], rot_c[:, :], vt3[:, :], Alu.subtract)
            tt_(vt3[:, :], vt2[:, :], rot_s[:, :], Alu.subtract)
            tt_(vt3[:, :], vt3[:, :], bt_f[:, :], Alu.mult)
            tt_(rot_s[:, :], rot_s[:, :], vt3[:, :], Alu.subtract)
        ve.tensor_copy(stepc[0][:, :], rot_c[:, :])
        ve.tensor_copy(steps[0][:, :], rot_s[:, :])
        for j in range(1, 9):
            tt_(vt1[:, :], stepc[j - 1][:, :], stepc[j - 1][:, :], Alu.mult)
            tt_(vt2[:, :], steps[j - 1][:, :], steps[j - 1][:, :], Alu.mult)
            tt_(stepc[j][:, :], vt1[:, :], vt2[:, :], Alu.subtract)
            tt_(vt1[:, :], stepc[j - 1][:, :], steps[j - 1][:, :], Alu.mult)
            ts_(steps[j][:, :], vt1[:, :], 2.0, None, Alu.mult)
        VE.inc()

    try:
        with nc.sbuf_tensor("wfcq", [128, DCH * HCH * 128], FP8) as wfcq, \
             nc.sbuf_tensor("wprjq", [128, HCH * D], BF16) as wprjq:
            with nc.sbuf_tensor("wpiece", [128, H], F32) as wpiece, \
                 nc.sbuf_tensor("wabs", [128, 1024], F32) as wabs, \
                 nc.sbuf_tensor("asum", [128, 64], F32) as asum, \
                 nc.sbuf_tensor("qtile", [128, 512], F32) as qtile, \
                 nc.sbuf_tensor("qpos", [128, 512], F32) as qpos, \
                 nc.sbuf_tensor("lnwb4", [128, 4 * HCH], BF16) as lnwb4, \
                 nc.sbuf_tensor("strow2", [1, 1024], F32) as strow2:

              def gamma_pass(w_dram, nrow, tau_col):
                  ncol = (H * D) // nrow
                  ac.wait_ge(VE.h, VE.n)
                  sync.wait_ge(VE.h, VE.n)
                  acc_cols = 0
                  for rc in range(nrow // 128):
                      n0 = LD.inc(sync.dma_start(out=wpiece[:, 0:ncol],
                                                 in_=w_dram[rc * 128:(rc + 1) * 128, :]), 16)
                      ac.wait_ge(LD.h, n0)
                      an = None
                      for gcol in range(ncol // 1024):
                          an = AC.inc(ac.activation(wabs[:, 0:1024],
                                                    wpiece[:, gcol * 1024:(gcol + 1) * 1024], AF.Abs,
                                                    accum_out=asum[:, acc_cols:acc_cols + 1]))
                          acc_cols += 1
                      sync.wait_ge(AC.h, an)
                  ve.wait_ge(AC.h, AC.n)
                  vn = VE.inc(ve.tensor_reduce(svec[:, 0:1], asum[:, 0:acc_cols],
                                               axis=mybir.AxisListType.X, op=Alu.add))
                  te.wait_ge(VE.h, vn)
                  pn = PE.inc(te.matmul(pb[6][0:1, 0:1], ones_col[:, :], svec[:, 0:1],
                                        start=True, stop=True))
                  ve.wait_ge(PE.h, pn)
                  vn = VE.inc(ve.tensor_copy(svec[0:1, 2:3], pb[6][0:1, 0:1]))
                  te.wait_ge(VE.h, vn)
                  pn = PE.inc(te.matmul(pb[6][:, 0:1], ones_row[:, :], svec[0:1, 2:3],
                                        start=True, stop=True))
                  ve.wait_ge(PE.h, pn)
                  ts_(s_tau[:, tau_col:tau_col + 1], pb[6][:, 0:1], 1.0 / (H * D), QEPS,
                      Alu.mult, Alu.add)
                  ts_(s_tau[:, tau_col:tau_col + 1], s_tau[:, tau_col:tau_col + 1], 0.5, None, Alu.mult)
                  VE.inc(ts_(s_tau[:, 2 + tau_col:3 + tau_col], s_tau[:, tau_col:tau_col + 1],
                             -1.0, None, Alu.mult))
                  te.wait_ge(VE.h, VE.n)

              gamma_pass(Wfc_d, H, 0)
              gamma_pass(Wproj_d, D, 1)

              def quant_transpose(w_dram, nrow, tau_col, dst_sb, dst_col_of):
                  ncol = (H * D) // nrow
                  for rc in range(nrow // 128):
                      sync.wait_ge(AC.h, AC.n)
                      sync.wait_ge(VE.h, VE.n)
                      n0 = LD.inc(sync.dma_start(out=wpiece[:, 0:ncol],
                                                 in_=w_dram[rc * 128:(rc + 1) * 128, :]), 16)
                      te.wait_ge(LD.h, n0)
                      for cc4 in range(0, ncol // 128, 4):
                          te.wait_ge(VE.h, VE.n)
                          for i in range(4):
                              cc = cc4 + i
                              PE.inc(trans(pb[4][:, i * 128:(i + 1) * 128],
                                           wpiece[:, cc * 128:(cc + 1) * 128],
                                           first=(i == 0), last=(i == 3)))
                          ve.wait_ge(PE.h, PE.n)
                          ts_(qpos[:, 0:512], pb[4][:, 0:512], s_tau[:, tau_col:tau_col + 1],
                              None, Alu.is_ge)
                          ts_(qtile[:, 0:512], pb[4][:, 0:512], s_tau[:, 2 + tau_col:3 + tau_col],
                              None, Alu.is_le)
                          for i in range(4):
                              cc = cc4 + i
                              dcol = dst_col_of(rc, cc)
                              VE.inc(tt_(dst_sb[:, dcol:dcol + 128], qpos[:, i * 128:(i + 1) * 128],
                                         qtile[:, i * 128:(i + 1) * 128], Alu.subtract))

              quant_transpose(Wfc_d, H, 0, wfcq, lambda rc, cc: (cc * HCH + rc) * 128)
              quant_transpose(Wproj_d, D, 1, wprjq, lambda rc, cc: cc * D + rc * 128)

              ve.wait_ge(LD.h, LD.n)
              for hc in range(HCH):
                  c0 = 4 * hc
                  ts_(lnwb4[:, c0:c0 + 1], lnw_sb[:, hc:hc + 1], 0.0, None, Alu.add)
                  stt_(lnwb4[:, c0 + 1:c0 + 2], lnw_sb[:, hc:hc + 1], -1.0,
                       lnwb4[:, c0:c0 + 1], Alu.mult, Alu.add)
                  ts_(lnwb4[:, c0 + 1:c0 + 2], lnwb4[:, c0 + 1:c0 + 2], -1.0, None, Alu.mult)
                  ts_(lnwb4[:, c0 + 2:c0 + 3], lnb_sb[:, hc:hc + 1], 0.0, None, Alu.add)
                  stt_(lnwb4[:, c0 + 3:c0 + 4], lnb_sb[:, hc:hc + 1], -1.0,
                       lnwb4[:, c0 + 2:c0 + 3], Alu.mult, Alu.add)
                  ts_(lnwb4[:, c0 + 3:c0 + 4], lnwb4[:, c0 + 3:c0 + 4], -1.0, None, Alu.mult)
              vn = VE.inc()
              te.wait_ge(VE.h, vn)
              for dh in range(2):
                  for which, dst in ((0, wW_b), (1, bW_b)):
                      for p2 in range(2):
                          col = which * 2 + p2
                          for hc in range(HCH):
                              PE.inc(te.matmul(pb[6][0:1, 0:512],
                                               lnwb4[:, 4 * hc + col:4 * hc + col + 1],
                                               wprjq[:, hc * D + dh * 512:hc * D + (dh + 1) * 512],
                                               start=(p2 == 0 and hc == 0),
                                               stop=(p2 == 1 and hc == HCH - 1)))
                      ve.wait_ge(PE.h, PE.n)
                      vn = VE.inc(ve.tensor_copy(strow2[0:1, 0:512], pb[6][0:1, 0:512]))
                      te.wait_ge(VE.h, vn)
                      pn = PE.inc(te.matmul(pb[6][:, 0:512], half_row[:, :],
                                            strow2[0:1, 0:512], start=True, stop=True))
                      ve.wait_ge(PE.h, pn)
                      vn = VE.inc(ve.tensor_copy(dst[:, dh * 512:(dh + 1) * 512], pb[6][:, 0:512]))
                      te.wait_ge(VE.h, vn)

              sync.wait_ge(VE.h, VE.n)
              LD.inc(sync.dma_start(out=wfcq_hbm[:, :], in_=wfcq[:, :]), 16)
              LD.inc(sync.dma_start(out=wprjq_hbm[:, :], in_=wprjq[:, :]), 16)
              with nc.sbuf_tensor("xcp", [128, D], F32) as xcp, \
                   nc.sbuf_tensor("xcs", [128, D], F32) as xcs:
                  for ch in range(NCH):
                      sync.wait_ge(VE.h, VE.n)
                      sync.wait_ge(LD.h, LD.n)
                      nx = LD.inc(sync.dma_start(out=xcp[:, :],
                                                 in_=x_in[ch * 128:(ch + 1) * 128, :]), 16)
                      ve.wait_ge(LD.h, nx)
                      ts_(xcs[:, :], xcp[:, :], 0.0, None, Alu.add, Alu.add,
                          accum_out=s_mu[:, ch:ch + 1])
                      vnx = VE.inc(stt_(xcs[:, :], xcp[:, :], 0.0, xcp[:, :],
                                        Alu.add, Alu.mult, accum_out=s_rs[:, ch:ch + 1]))
                      sync.wait_ge(VE.h, vnx)
                      LD.inc(sync.dma_start(out=x_hbm[ch * 128:(ch + 1) * 128, :],
                                            in_=xcp[:, :]), 16)
                  sync.wait_ge(LD.h, LD.n)
                  ve.wait_ge(LD.h, LD.n)
                  gp.wait_ge(LD.h, LD.n)
              with nc.sbuf_tensor("zer", [128, ACCW], F32) as zer:
                  vn = VE.inc(ve.memset(zer[:, :], 0.0))
                  sync.wait_ge(VE.h, vn)
                  for i in range(TPC // 128):
                      LD.inc(sync.dma_start(out=acc_hbm[i * 128:(i + 1) * 128, :], in_=zer[:, :]), 16)
              barrier()

            if stop_after != "prep":
                ffn_pass(wfcq, wprjq, write_y=False, tag="a",
                     stats_mu=s_mu, stats_rs=s_rs, dst_mu=s_mu2, dst_rs=s_rs2)
            if stop_after in ("prep", "ffn1"):
                with nc.sbuf_tensor("dmp", [128, GRP * D], F32) as dmp:
                    for g in range(NG):
                        n0 = LD.inc(sync.dma_start(
                            out=dmp[:, :].rearrange("p (a c) -> p a c", a=GRP),
                            in_=x_hbm[g * GRP * 128:(g + 1) * GRP * 128, :].rearrange(
                                "(a p) c -> p a c", p=128)), 16)
                        sync.wait_ge(LD.h, n0)
                        n1 = LD.inc(sync.dma_start(
                            out=y_out[g * GRP * 128:(g + 1) * GRP * 128, :].rearrange(
                                "(a p) c -> p a c", p=128),
                            in_=dmp[:, :].rearrange("p (a c) -> p a c", a=GRP)), 16)
                        sync.wait_ge(LD.h, n1)
                raise _Stop


        with ExitStack() as p3ctx:
            E3 = p3ctx.enter_context
            w3h = E3(nc.sbuf_tensor("w3h", [128, 3 * DCH * D], BF16))
            w3l = E3(nc.sbuf_tensor("w3l", [128, 3 * DCH * D], BF16))
            xg2 = E3(nc.sbuf_tensor("xg2", [128, GRP * D], F32))
            xsc2 = E3(nc.sbuf_tensor("xsc2", [128, GRP * D], F32))
            ln2Th = E3(nc.sbuf_tensor("ln2Th", [128, DCH * 128], BF16))
            ln2Tl = E3(nc.sbuf_tensor("ln2Tl", [128, DCH * 128], BF16))
            kqv = E3(nc.sbuf_tensor("kqv", [128, 3 * D], F32))
            krot = E3(nc.sbuf_tensor("krot", [128, 2 * D], F32))
            tabc = E3(nc.sbuf_tensor("tabc", [128, 512], F32))
            tabs = E3(nc.sbuf_tensor("tabs", [128, 512], F32))
            rtmp = E3(nc.sbuf_tensor("rtmp", [128, 512], F32))
            rtmp2 = E3(nc.sbuf_tensor("rtmp2", [128, 512], F32))
            tph = E3(nc.sbuf_tensor("tph", [128, 8 * 128], BF16))
            tpl = E3(nc.sbuf_tensor("tpl", [128, 8 * 128], BF16))
            HD2 = DCH // 2
            with nc.sbuf_tensor("wtmp", [128, HD2 * D], F32) as wtmp:
                for wi, wd in enumerate((Wk_d, Wq_d, Wv_d)):
                    for hf in range(2):
                        sync.wait_ge(VE.h, VE.n)
                        n0 = LD.inc(sync.dma_start(
                            out=wtmp[:, :].rearrange("p (a c) -> p a c", a=HD2),
                            in_=wd[hf * HD2 * 128:(hf + 1) * HD2 * 128, :].rearrange(
                                "(a p) c -> p a c", p=128)), 16)
                        ve.wait_ge(LD.h, n0)
                        o = wi * DCH * D + hf * HD2 * D
                        ve.tensor_copy(w3h[:, o:o + HD2 * D], wtmp[:, :])
                        VE.inc(tt_(w3l[:, o:o + HD2 * D], wtmp[:, :], w3h[:, o:o + HD2 * D],
                                   Alu.subtract))
                VE.inc()
                ve.wait_ge(VE.h, VE.n)
            ln_finish(s_mu2[:, 0:NCH], s_rs2[:, 0:NCH])
            for g in range(NG):
                if g == NG // 2:
                    gp.wait_ge(LD.h, LD.n)
                    gp.collective_compute("AllGather", Alu.bypass,
                                          replica_groups=[list(range(NCORES))],
                                          ins=[qb_in[0:2 * D, :].opt()],
                                          outs=[qb_out[0:NCORES * 2 * D, :].opt()]
                                          ).then_inc(CC.h, 1)
                    CC.n += 1
                    gp.collective_compute("AllGather", Alu.bypass,
                                          replica_groups=[list(range(NCORES))],
                                          ins=[vb_in[0:TPC, :].opt()],
                                          outs=[vb_out[0:NCORES * TPC, :].opt()]
                                          ).then_inc(CC.h, 1)
                    CC.n += 1
                ch0 = g * GRP
                sync.wait_ge(VE.h, VE.n)
                sync.wait_ge(LD.h, LD.n)
                n0 = LD.inc(sync.dma_start(
                    out=xg2[:, :].rearrange("p (a c) -> p a c", a=GRP),
                    in_=x_hbm[ch0 * 128:(ch0 + GRP) * 128, :].rearrange(
                        "(a p) c -> p a c", p=128)), 16)
                ve.wait_ge(LD.h, n0)
                for i in range(GRP):
                    ch = ch0 + i
                    blk_i = ch // NIC
                    tok0 = (ch % NIC) * 128
                    vn = VE.inc(ts_(xsc2[:, i * D:(i + 1) * D], xg2[:, i * D:(i + 1) * D],
                                    s_mu2[:, ch:ch + 1], s_rs2[:, ch:ch + 1],
                                    Alu.subtract, Alu.mult))
                    te.wait_ge(VE.h, vn)
                    for dq in range(0, DCH, 4):
                        for t4 in range(4):
                            dc = dq + t4
                            PE.inc(trans(pb[4][:, t4 * 128:(t4 + 1) * 128],
                                         xsc2[:, i * D + dc * 128:i * D + (dc + 1) * 128],
                                         first=(t4 == 0), last=(t4 == 3)))
                        ve.wait_ge(PE.h, PE.n)
                        vn = None
                        for t4 in range(4):
                            dc = dq + t4
                            ve.tensor_copy(ln2Th[:, dc * 128:(dc + 1) * 128],
                                           pb[4][:, t4 * 128:(t4 + 1) * 128])
                            vn = VE.inc(tt_(ln2Tl[:, dc * 128:(dc + 1) * 128],
                                            pb[4][:, t4 * 128:(t4 + 1) * 128],
                                            ln2Th[:, dc * 128:(dc + 1) * 128], Alu.subtract))
                        te.wait_ge(VE.h, vn)
                    ve.wait_ge(LD.h, LD.n)
                    for wi in range(3):
                        for dh in range(2):
                            bank = pb[(wi * 2 + dh) % 4]
                            wof = wi * DCH * D + dh * 512
                            np_ = 0
                            passes = ((ln2Th, w3h), (ln2Th, w3l), (ln2Tl, w3h))
                            for pi, (lt, wt) in enumerate(passes):
                                for dc in range(DCH):
                                    np_ += 1
                                    PE.inc(te.matmul(bank[:, 0:512],
                                                     lt[:, dc * 128:(dc + 1) * 128],
                                                     wt[:, wof + dc * D:wof + dc * D + 512],
                                                     start=(pi == 0 and dc == 0),
                                                     stop=(pi == 2 and dc == DCH - 1)))
                            pn = PE.n
                            ve.wait_ge(PE.h, pn)
                            dst = kqv[:, wi * D + dh * 512:wi * D + (dh + 1) * 512]
                            if wi == 0:
                                vn = VE.inc(tt_(dst, bank[:, 0:512],
                                                bk_b[:, dh * 512:(dh + 1) * 512], Alu.add))
                            elif wi == 1:
                                vn = VE.inc(tt_(dst, bank[:, 0:512],
                                                bq_b[:, dh * 512:(dh + 1) * 512], Alu.add))
                            else:
                                vn = VE.inc(ve.tensor_copy(dst, bank[:, 0:512]))
                            te.wait_ge(VE.h, vn)
                    ve.memset(tabc[:, 0:1], 1.0)
                    ve.memset(tabs[:, 0:1], 0.0)
                    for r in range(9):
                        n = 1 << r
                        sc = stepc[r][:, ch:ch + 1]; ss = steps[r][:, ch:ch + 1]
                        ts_(rtmp[:, 0:n], tabc[:, 0:n], sc, None, Alu.mult)
                        ts_(rtmp2[:, 0:n], tabs[:, 0:n], ss, None, Alu.mult)
                        tt_(tabc[:, n:2 * n], rtmp[:, 0:n], rtmp2[:, 0:n], Alu.subtract)
                        ts_(rtmp[:, 0:n], tabc[:, 0:n], ss, None, Alu.mult)
                        stt_(tabs[:, n:2 * n], tabs[:, 0:n], sc, rtmp[:, 0:n], Alu.mult, Alu.add)
                    for t_i in range(2):
                        srcv = kqv[:, t_i * D:(t_i + 1) * D]
                        dstv = krot[:, t_i * D:(t_i + 1) * D]
                        z0 = srcv[:, 0:D:2]; z1 = srcv[:, 1:D:2]
                        r0 = dstv[:, 0:D:2]; r1 = dstv[:, 1:D:2]
                        tt_(rtmp[:, 0:512], tabc[:, :], z0, Alu.mult)
                        tt_(rtmp2[:, 0:512], tabs[:, :], z1, Alu.mult)
                        tt_(r0, rtmp[:, 0:512], rtmp2[:, 0:512], Alu.add)
                        tt_(rtmp[:, 0:512], tabs[:, :], z0, Alu.mult)
                        tt_(rtmp2[:, 0:512], tabc[:, :], z1, Alu.mult)
                        VE.inc(tt_(r1, rtmp2[:, 0:512], rtmp[:, 0:512], Alu.subtract))
                    for t_i, dst_hbm in ((0, kT_hbm), (1, qb_in)):
                        te.wait_ge(VE.h, VE.n)
                        ve.wait_ge(LD.h, LD.n)
                        for dq in range(0, DCH, 4):
                            for t4 in range(4):
                                dc = dq + t4
                                PE.inc(trans(pb[4][:, t4 * 128:(t4 + 1) * 128],
                                             krot[:, t_i * D + dc * 128:t_i * D + (dc + 1) * 128],
                                             first=(t4 == 0), last=(t4 == 3)))
                            ve.wait_ge(PE.h, PE.n)
                            vn = None
                            for t4 in range(4):
                                dc = dq + t4
                                ve.tensor_copy(tph[:, dc * 128:(dc + 1) * 128],
                                               pb[4][:, t4 * 128:(t4 + 1) * 128])
                                vn = VE.inc(tt_(tpl[:, dc * 128:(dc + 1) * 128],
                                                pb[4][:, t4 * 128:(t4 + 1) * 128],
                                                tph[:, dc * 128:(dc + 1) * 128], Alu.subtract))
                            te.wait_ge(VE.h, vn)
                        sync.wait_ge(VE.h, VE.n)
                        for sec, tpx in ((0, tph), (1, tpl)):
                            base = (blk_i * 2 * D + sec * D) if t_i == 1 else (sec * 2 * D + blk_i * D)
                            for dc in range(DCH):
                                LD.inc(sync.dma_start(
                                    out=dst_hbm[base + dc * 128:base + (dc + 1) * 128,
                                                tok0:tok0 + 128],
                                    in_=tpx[:, dc * 128:(dc + 1) * 128]), 16)
                    vsrc = kqv[:, 2 * D:3 * D]
                    ve.wait_ge(LD.h, LD.n)
                    ve.tensor_copy(tph[:, 0:D], vsrc)
                    VE.inc(tt_(tpl[:, 0:D], vsrc, tph[:, 0:D], Alu.subtract))
                    sync.wait_ge(VE.h, VE.n)
                    vrow0 = blk_i * TPC + tok0
                    LD.inc(sync.dma_start(out=vb_in[vrow0:vrow0 + 128, :],
                                          in_=tph[:, 0:D]), 16)
                    LD.inc(sync.dma_start(out=vb_in[vrow0 + BLK:vrow0 + BLK + 128, :],
                                          in_=tpl[:, 0:D]), 16)
            barrier()

        if stop_after == "qkv":
            raise _Stop
        gp.wait_ge(LD.h, LD.n)
        gp.collective_compute("AllGather", Alu.bypass,
                              replica_groups=[list(range(NCORES))],
                              ins=[qb_in[2 * D:4 * D, :].opt()],
                              outs=[qb_out[NCORES * 2 * D:2 * NCORES * 2 * D, :].opt()]
                              ).then_inc(CC.h, 1)
        CC.n += 1
        gp.collective_compute("AllGather", Alu.bypass,
                              replica_groups=[list(range(NCORES))],
                              ins=[vb_in[TPC:2 * TPC, :].opt()],
                              outs=[vb_out[NCORES * TPC:2 * NCORES * TPC, :].opt()]
                              ).then_inc(CC.h, 1)
        CC.n += 1
        for eng in (sync, gp, te, ve, ac):
            for s in (LD, GD, PE, VE, AC, GP, KQ, VB, SC):
                if s.n:
                    eng.wait_ge(s.h, s.n)

        with ExitStack() as p5ctx:
            E5 = p5ctx.enter_context
            kTbh = E5(nc.sbuf_tensor("kTbh", [128, DCH * BLK], BF16))
            kTbl = E5(nc.sbuf_tensor("kTbl", [128, DCH * BLK], BF16))
            qTbh = E5(nc.sbuf_tensor("qTbh", [128, DCH * BLK], BF16))
            qTbl = E5(nc.sbuf_tensor("qTbl", [128, DCH * BLK], BF16))
            vbh = E5(nc.sbuf_tensor("vbh", [128, NIC * D], BF16))
            vbl = E5(nc.sbuf_tensor("vbl", [128, NIC * D], BF16))
            weih = E5(nc.sbuf_tensor("weih", [128, NJC * BLK], BF16))
            weil = E5(nc.sbuf_tensor("weil", [128, NJC * BLK], BF16))
            weif = E5(nc.sbuf_tensor("weif", [128, ICOL], F32))
            ones_bf = E5(nc.sbuf_tensor("ones_bf", [128, 1], BF16))
            mtile = E5(nc.sbuf_tensor("mtile", [128, ICOL], F32))
            stg0 = E5(nc.sbuf_tensor("stg0", [128, NIC * ACCW], F32))
            stgs = [stg0]
            ve.memset(ones_bf[:, :], 1.0)
            for sb in stgs:
                for ic in range(NIC):
                    ve.memset(sb[:, ic * ACCW + 1025:(ic + 1) * ACCW], 0.0)
            VE.inc()
            gp.wait_ge(VE.h, VE.n)
            kq_loaded = [0] * 17
            v_loaded = [0] * 17
            scores_done = [0] * 17
            wv_done = [0] * 17
            stg_free = [0]
            sp_done = [0] * 17

            def gather_kq(p):
                for sec, (kdst, qdst) in enumerate(((kTbh, qTbh), (kTbl, qTbl))):
                    for cc in range(DCH):
                        col = (p * DCH + cc) * 2 + sec
                        KQ.inc(gp.indirect_dma_start(
                            kdst[:, cc * BLK:(cc + 1) * BLK], None, kT_hbm[:, :],
                            IndirectOffsetOnAxis(ap=s_kidx[:, col:col + 1], axis=0)), 16)
                        KQ.inc(gp.indirect_dma_start(
                            qdst[:, cc * BLK:(cc + 1) * BLK], None, qb_out[:, :],
                            IndirectOffsetOnAxis(ap=s_qidx[:, col:col + 1], axis=0)), 16)
                return KQ.n

            def gather_v(p):
                for sec, vdst in enumerate((vbh, vbl)):
                    for tc in range(NIC):
                        col = (p * NIC + tc) * 2 + sec
                        VB.inc(gp.indirect_dma_start(
                            vdst[:, tc * D:(tc + 1) * D], None, vb_out[:, :],
                            IndirectOffsetOnAxis(ap=s_vidx[:, col:col + 1], axis=0)), 16)
                return VB.n

            gp.wait_ge(CC.h, 2)
            kq_loaded[0] = gather_kq(0)
            v_loaded[0] = gather_v(0)

            bankf = {0: 0, 1: 0, 2: 0, 3: 0}
            bank5_f = 0
            wv_pe_prev = 0
            for p in range(17):
                te.wait_ge(KQ.h, kq_loaded[p])
                ve.wait_ge(PE.h, wv_pe_prev)
                spn = None
                for jj in range(NJC):
                    for ih in range(NIT):
                        b = (jj * NIT + ih) % 2
                        bank = pb[b]
                        te.wait_ge(VE.h, bankf[b])
                        passes = ((qTbh, kTbh), (qTbh, kTbl), (qTbl, kTbh))
                        for pi, (qs, ks) in enumerate(passes):
                            for cc in range(DCH):
                                PE.inc(te.matmul(
                                    bank[:, 0:ICOL],
                                    qs[:, cc * BLK + jj * 128:cc * BLK + (jj + 1) * 128],
                                    ks[:, cc * BLK + ih * ICOL:cc * BLK + (ih + 1) * ICOL],
                                    start=(pi == 0 and cc == 0),
                                    stop=(pi == 2 and cc == DCH - 1)))
                        pn = PE.n
                        off = jj * 128 - ih * ICOL
                        y0 = BLK - off
                        ve.wait_ge(PE.h, pn)
                        ts_(mtile[:, 0:ICOL], miota[:, y0:y0 + ICOL], s_dneg[:, p:p + 1],
                            NEG_BIG, Alu.is_lt, Alu.mult)
                        vn = VE.inc(tt_(weif[:, 0:ICOL], bank[:, 0:ICOL],
                                        mtile[:, 0:ICOL], Alu.add))
                        bankf[b] = vn
                        ac.wait_ge(VE.h, vn)
                        ac.activation(weif[:, 0:ICOL], weif[:, 0:ICOL],
                                      AF.Exp, scale=INV_SQRT_C)
                        an = AC.inc(ac.activation(weif[:, 0:ICOL], weif[:, 0:ICOL],
                                                  AF.Ln, bias=1.0, scale=1.0))
                        wcol = jj * BLK + ih * ICOL
                        ve.wait_ge(AC.h, an)
                        ve.tensor_copy(weih[:, wcol:wcol + ICOL], weif[:, 0:ICOL])
                        spn = VE.inc(tt_(weil[:, wcol:wcol + ICOL], weif[:, 0:ICOL],
                                         weih[:, wcol:wcol + ICOL], Alu.subtract))
                sp_done[p] = spn
                scores_done[p] = PE.n
                if p + 1 < 17:
                    gp.wait_ge(PE.h, scores_done[p])
                    if p + 1 == 9:
                        gp.wait_ge(CC.h, 4)
                    kq_loaded[p + 1] = gather_kq(p + 1)
                stg = stgs[0]
                te.wait_ge(VE.h, sp_done[p])
                te.wait_ge(VB.h, v_loaded[p])
                ve.wait_ge(SC.h, stg_free[0])
                te.wait_ge(VE.h, bank5_f)
                first_R = True
                for ic in range(NIC):
                    for dh in range(2):
                        b = 2 + (ic * 2 + dh) % 2
                        bank = pb[b]
                        te.wait_ge(VE.h, bankf[b])
                        passes = ((weih, vbh), (weih, vbl), (weil, vbh))
                        for pi, (ws, vs) in enumerate(passes):
                            for jj in range(NJC):
                                PE.inc(te.matmul(
                                    bank[:, 0:512],
                                    ws[:, jj * BLK + ic * 128:jj * BLK + (ic + 1) * 128],
                                    vs[:, jj * D + dh * 512:jj * D + (dh + 1) * 512],
                                    start=(pi == 0 and jj == 0),
                                    stop=(pi == 2 and jj == NJC - 1)))
                                if dh == 0 and pi <= 1:
                                    wsr = (weih, weil)[pi]
                                    PE.inc(te.matmul(
                                        pb[5][:, ic:ic + 1],
                                        wsr[:, jj * BLK + ic * 128:jj * BLK + (ic + 1) * 128],
                                        ones_bf[:, :],
                                        start=first_R, stop=False,
                                        skip_group_check=True))
                                    first_R = False
                        pn = PE.n
                        ve.wait_ge(PE.h, pn)
                        vn = VE.inc(ve.tensor_copy(
                            stg[:, ic * ACCW + dh * 512:ic * ACCW + (dh + 1) * 512],
                            bank[:, 0:512]))
                        bankf[b] = vn
                ve.wait_ge(PE.h, PE.n)
                vn = VE.inc(ve.tensor_copy(stg[:, 1024:(NIC - 1) * ACCW + 1025:ACCW],
                                           pb[5][:, 0:NIC]))
                bank5_f = vn
                wv_done[p] = VE.n
                if p + 1 < 17:
                    gp.wait_ge(VE.h, wv_done[p])
                    v_loaded[p + 1] = gather_v(p + 1)
                gp.wait_ge(VE.h, wv_done[p])
                for ic in range(NIC):
                    SC.inc(gp.indirect_dma_start(
                        acc_hbm[:, :],
                        IndirectOffsetOnAxis(ap=s_aidx[:, p * NIC + ic:p * NIC + ic + 1], axis=0),
                        stg[:, ic * ACCW:(ic + 1) * ACCW], None,
                        compute_op=Alu.add), 16)
                    gp.wait_ge(SC.h, SC.n)
                stg_free[0] = SC.n
                wv_pe_prev = PE.n
            barrier()

        if stop_after == "attn":
            raise _Stop
        with nc.sbuf_tensor("wp", [128, DCH * D], F32) as wp, \
             nc.sbuf_tensor("accs", [128, ACCW], F32) as accs, \
             nc.sbuf_tensor("att", [128, D], F32) as att, \
             nc.sbuf_tensor("attT", [128, DCH * 128], F32) as attT, \
             nc.sbuf_tensor("xg3", [128, D], F32) as xg3:
            LD.inc(sync.dma_start(out=wp[:, :].rearrange("p (a c) -> p a c", a=DCH),
                                  in_=Wp_d.rearrange("(a p) c -> p a c", p=128)), 16)
            ld_wp = LD.n
            te.wait_ge(LD.h, ld_wp)
            for ch in range(NCH):
                sync.wait_ge(VE.h, VE.n)
                sync.wait_ge(LD.h, LD.n)
                n0 = LD.inc(sync.dma_start(out=accs[:, :],
                                           in_=acc_hbm[ch * 128:(ch + 1) * 128, :]), 16)
                n1 = LD.inc(sync.dma_start(out=xg3[:, :],
                                           in_=x_hbm[ch * 128:(ch + 1) * 128, :]), 16)
                ve.wait_ge(LD.h, n1)
                stt_(att[:, 0:512], bv_b[:, 0:512], accs[:, 1024:1025], accs[:, 0:512],
                     Alu.mult, Alu.add)
                vn = VE.inc(stt_(att[:, 512:1024], bv_b[:, 512:1024], accs[:, 1024:1025],
                                 accs[:, 512:1024], Alu.mult, Alu.add))
                te.wait_ge(VE.h, vn)
                for dq in range(0, DCH, 4):
                    for t4 in range(4):
                        dc = dq + t4
                        PE.inc(trans(pb[4][:, t4 * 128:(t4 + 1) * 128],
                                     att[:, dc * 128:(dc + 1) * 128],
                                     first=(t4 == 0), last=(t4 == 3)))
                    ve.wait_ge(PE.h, PE.n)
                    vn = None
                    for t4 in range(4):
                        vn = VE.inc(ve.tensor_copy(attT[:, (dq + t4) * 128:(dq + t4 + 1) * 128],
                                                   pb[4][:, t4 * 128:(t4 + 1) * 128]))
                    te.wait_ge(VE.h, vn)
                for dh in range(2):
                    bank = pb[dh]
                    for dc in range(DCH):
                        PE.inc(te.matmul(bank[:, 0:512], attT[:, dc * 128:(dc + 1) * 128],
                                         wp[:, dc * D + dh * 512:dc * D + (dh + 1) * 512],
                                         start=(dc == 0), stop=(dc == DCH - 1)))
                    pn = PE.n
                    ve.wait_ge(PE.h, pn)
                    xs = xg3[:, dh * 512:(dh + 1) * 512]
                    tt_(xs, xs, bank[:, 0:512], Alu.add)
                    vn = VE.inc(tt_(xs, xs, bp_b[:, dh * 512:(dh + 1) * 512], Alu.add))
                    te.wait_ge(VE.h, vn)
                ts_(att[:, :], xg3[:, :], 0.0, None, Alu.add, Alu.add,
                    accum_out=s_mu[:, ch:ch + 1])
                stt_(att[:, :], xg3[:, :], 0.0, xg3[:, :], Alu.add, Alu.mult,
                     accum_out=s_rs[:, ch:ch + 1])
                VE.inc()
                sync.wait_ge(VE.h, VE.n)
                LD.inc(sync.dma_start(out=x_hbm[ch * 128:(ch + 1) * 128, :], in_=xg3[:, :]), 16)
            barrier()

        if stop_after == "pproj":
            raise _Stop
        with nc.sbuf_tensor("wfcq2", [128, DCH * HCH * 128], FP8) as wfcq2, \
             nc.sbuf_tensor("wprjq2", [128, HCH * D], BF16) as wprjq2:
            LD.inc(sync.dma_start(out=wfcq2[:, :], in_=wfcq_hbm[:, :]), 16)
            LD.inc(sync.dma_start(out=wprjq2[:, :], in_=wprjq_hbm[:, :]), 16)
            wld = LD.n
            te.wait_ge(LD.h, wld)
            ve.wait_ge(LD.h, wld)

            ffn_pass(wfcq2, wprjq2, write_y=True, tag="b",
                 stats_mu=s_mu, stats_rs=s_rs)

    except _Stop:
        pass

    sync.wait_ge(LD.h, LD.n)
    sync.wait_ge(GD.h, GD.n)
    sync.wait_ge(SC.h, SC.n)
    try:
        octx.close()
    except AssertionError:
        pass
    return nc



def _pairs_for_core(c):
    bA, bB = c, NBLK - 1 - c
    return ([(J, 0, bA) for J in range(bA + 1)]
            + [(J, 1, bB) for J in range(bB + 1)])


def _core_tables(c, cfg):
    BLK, NIC = cfg.BLK, cfg.NIC
    pairs = _pairs_for_core(c)
    assert len(pairs) == 17
    pp = np.arange(128)
    qidx = np.zeros((128, 17 * DCH * 2), np.int32)
    kidx = np.zeros((128, 17 * DCH * 2), np.int32)
    vidx = np.zeros((128, 17 * NIC * 2), np.int32)
    aidx = np.zeros((128, 17 * NIC), np.int32)
    dneg = np.zeros((128, 17), np.float32)
    for p, (J, s_I, I_blk) in enumerate(pairs):
        oJ = J if J < NCORES else NBLK - 1 - J
        sJ = 0 if J < NCORES else 1
        for cc in range(DCH):
            for sec in range(2):
                qidx[:, (p * DCH + cc) * 2 + sec] = (
                    (sJ * NCORES + oJ) * 2 * D + sec * D + cc * 128 + pp)
                kidx[:, (p * DCH + cc) * 2 + sec] = (
                    sec * 2 * D + s_I * D + cc * 128 + pp)
        for tc in range(NIC):
            for sec in range(2):
                vidx[:, (p * NIC + tc) * 2 + sec] = (
                    (sJ * NCORES + oJ) * cfg.TPC + sec * BLK + tc * 128 + pp)
            aidx[:, p * NIC + tc] = s_I * BLK + tc * 128 + pp
        dneg[:, p] = -float((I_blk - J) * BLK)
    return qidx, kidx, vidx, aidx, dneg


def _shard_host(cfg, inputs):
    T = cfg.T
    x = np.ascontiguousarray(np.asarray(inputs["L0_attn_x"], np.float32).reshape(T, D))
    lnw = np.ascontiguousarray(np.asarray(inputs["L0_ffn_ln_w"], np.float32).reshape(HCH, 128).T)
    lnb = np.ascontiguousarray(np.asarray(inputs["L0_ffn_ln_b"], np.float32).reshape(HCH, 128).T)
    common = {
        "Wfc": np.ascontiguousarray(np.asarray(inputs["L0_ffn_W_fc"], np.float32)),
        "Wproj": np.ascontiguousarray(np.asarray(inputs["L0_ffn_W_proj"], np.float32)),
        "Wk": np.ascontiguousarray(np.asarray(inputs["L0_attn_Wk"], np.float32)),
        "Wq": np.ascontiguousarray(np.asarray(inputs["L0_attn_Wq"], np.float32)),
        "Wv": np.ascontiguousarray(np.asarray(inputs["L0_attn_Wv"], np.float32)),
        "Wp": np.ascontiguousarray(np.asarray(inputs["L0_attn_Wp"], np.float32)),
        "bk": np.ascontiguousarray(np.asarray(inputs["L0_attn_bk"], np.float32).reshape(1, D)),
        "bq": np.ascontiguousarray(np.asarray(inputs["L0_attn_bq"], np.float32).reshape(1, D)),
        "bv": np.ascontiguousarray(np.asarray(inputs["L0_attn_bv"], np.float32).reshape(1, D)),
        "bp": np.ascontiguousarray(np.asarray(inputs["L0_attn_bp"], np.float32).reshape(1, D)),
        "lnw_t": lnw, "lnb_t": lnb,
    }
    in_maps = []
    for c in range(NCORES):
        bA, bB = c, NBLK - 1 - c
        xs = np.concatenate([x[bA * cfg.BLK:(bA + 1) * cfg.BLK],
                             x[bB * cfg.BLK:(bB + 1) * cfg.BLK]], axis=0)
        qidx, kidx, vidx, aidx, dneg = _core_tables(c, cfg)
        toks = np.concatenate([np.arange(bA * cfg.BLK, (bA + 1) * cfg.BLK),
                               np.arange(bB * cfg.BLK, (bB + 1) * cfg.BLK)])
        m = (cfg.T - 1 - toks).astype(np.int32).reshape(cfg.NCH, 128).T
        im = dict(common)
        im["x_in"] = np.ascontiguousarray(xs)
        im["m_vals"] = np.ascontiguousarray(m)
        im["dneg"] = dneg
        im["qidx"] = qidx; im["kidx"] = kidx; im["vidx"] = vidx; im["aidx"] = aidx
        in_maps.append(im)
    return in_maps


def _gather_host(cfg, results):
    T = cfg.T
    y = np.zeros((T, D), np.float32)
    for c in range(NCORES):
        bA, bB = c, NBLK - 1 - c
        yc = results[c]["y"]
        y[bA * cfg.BLK:(bA + 1) * cfg.BLK] = yc[:cfg.BLK]
        y[bB * cfg.BLK:(bB + 1) * cfg.BLK] = yc[cfg.BLK:]
    return y.reshape(1, T, D)


def kernel(**inputs):
    T = int(np.asarray(inputs["L0_attn_x"]).shape[1])
    cfg = Cfg(T)
    nc = build_program(cfg)
    in_maps = _shard_host(cfg, inputs)
    res = run_bass_kernel_spmd(nc, in_maps, core_ids=list(range(NCORES)))
    return _gather_host(cfg, res.results)
